# version 05
# baseline (speedup 1.0000x reference)
import math
import numpy as np
import ml_dtypes
import concourse.bass as bass
import concourse.mybir as mybir
from concourse.bass_utils import run_bass_kernel_spmd
from contextlib import ExitStack

F32 = mybir.dt.float32
BF16 = mybir.dt.bfloat16
I8 = mybir.dt.int8
AF = mybir.ActivationFunctionType
ALU = mybir.AluOpType
AX = mybir.AxisListType
BF = ml_dtypes.bfloat16

D = 2048
NKI = 16
T = 2048
NB = 2
NT = NB * T
NTG = 8
H = 16
HD = 128
INTER = 8192
I2 = 1024
NE = 8
NEXP = 12
SCALE_ATT = 1.0 / math.sqrt(128.0)
ENGINES = ("tensor", "vector", "scalar", "gpsimd", "sync")


class Prog:
    def __init__(self):
        self.ops = []

    def _op(self, engine, fn, r, w, dma, sync=False, chain=None):
        r = tuple(r)
        w = tuple(w)
        psr = tuple(k for k in r if k.startswith("ps"))
        if psr:
            r = tuple(k for k in r if not k.startswith("ps"))
            w = w + psr
        self.ops.append([engine, fn, r, w, dma, False, sync, chain])

    def t(self, fn, r=(), w=(), sync=False):
        self._op("tensor", fn, r, w, False, sync)

    def v(self, fn, r=(), w=(), sync=False):
        self._op("vector", fn, r, w, False, sync)

    def s(self, fn, r=(), w=(), sync=False):
        self._op("scalar", fn, r, w, False, sync)

    def g(self, fn, r=(), w=(), sync=False):
        self._op("gpsimd", fn, r, w, False, sync)

    def dma(self, queue, fn, r=(), w=(), chain=None):
        self._op(queue, fn, r, w, True, chain=chain)

    def barrier(self):
        self.ops.append(["BAR", None, (), (), False, False, False, None])

    def schedule(self):
        n = len(self.ops)
        deps = [set() for _ in range(n)]
        last_w = {}
        readers = {}
        last_on = {}
        bar_deps = []
        bar_edges = set()
        seen_since_bar = set()
        for i, op in enumerate(self.ops):
            eng, _fn, rds, wrs, dma, _, syncf = op[:7]
            if eng == "BAR":
                bar_deps = [j for j in last_on.values() if j is not None]
                seen_since_bar = set()
                continue
            if eng not in seen_since_bar:
                for j in bar_deps:
                    if self.ops[j][0] != eng or self.ops[j][4] or dma:
                        deps[i].add(j)
                        bar_edges.add((i, j))
                seen_since_bar.add(eng)
            for r in rds:
                if r in last_w:
                    deps[i].add(last_w[r])
                readers.setdefault(r, []).append(i)
            for w in wrs:
                if w in last_w:
                    deps[i].add(last_w[w])
                for j in readers.get(w, ()):
                    deps[i].add(j)
            if not syncf:
                deps[i] = {j for j in deps[i]
                           if not (self.ops[j][0] == eng and not self.ops[j][4]
                                   and not dma)}
            for w in wrs:
                last_w[w] = i
                readers[w] = []
            last_on[eng] = i
        chains = {}
        for idx, op in enumerate(self.ops):
            if op[0] != "BAR" and op[7] is not None:
                chains.setdefault(op[7], []).append(idx)
        nxt = {}
        import os as _os
        if _os.environ.get("NOCHAIN", "") != "1":
            for lst in chains.values():
                for a, b in zip(lst, lst[1:]):
                    nxt[a] = b
        for i, op in enumerate(self.ops):
            if op[0] == "BAR":
                continue
            nd = set()
            for j in deps[i]:
                cj = self.ops[j][7]
                if (cj is not None and op[7] != cj and j in nxt
                        and (i, j) not in bar_edges):
                    nd.add(nxt[j])
                else:
                    nd.add(j)
            best = {}
            for j in nd:
                se = self.ops[j][0]
                if se not in best or j > best[se]:
                    best[se] = j
            deps[i] = set(best.values())
            for j in deps[i]:
                self.ops[j][5] = True
        cnt = {e: 0 for e in ENGINES}
        count_after = {}
        for i, op in enumerate(self.ops):
            eng = op[0]
            if eng == "BAR":
                continue
            if op[4]:
                cnt[eng] += 16
                count_after[i] = cnt[eng]
            elif op[5]:
                cnt[eng] += 1
                count_after[i] = cnt[eng]
        self.deps = deps
        self.count_after = count_after
        self.totals = cnt

    def emit(self, nc, stack):
        self.schedule()
        sems = {e: stack.enter_context(nc.semaphore(f"sem_{e}")) for e in ENGINES}
        blk = stack.enter_context(nc.Block())
        per_engine = {e: [] for e in ENGINES}
        for i, op in enumerate(self.ops):
            if op[0] != "BAR":
                per_engine[op[0]].append(i)

        def run_engine(eng_name):
            def body(eng):
                waited = {e: 0 for e in ENGINES}
                for i in per_engine[eng_name]:
                    op = self.ops[i]
                    ds = sorted(self.deps[i], key=lambda j: self.count_after[j])
                    fresh_compute = False
                    for j in ds:
                        src = self.ops[j][0]
                        need = self.count_after[j]
                        if waited[src] < need:
                            eng.wait_ge(sems[src], need)
                            waited[src] = need
                            if src in ("vector", "scalar", "gpsimd") \
                                    and not self.ops[j][4]:
                                fresh_compute = True
                    if op[4] and fresh_compute:
                        eng.nop(cycle_cnt=700, nofuse=True)
                    inst = op[1](eng)
                    if op[4]:
                        inst.then_inc(sems[eng_name], 16)
                    elif op[5]:
                        inst.then_inc(sems[eng_name], 1)
                if eng_name == "sync":
                    for q in ENGINES:
                        if self.totals[q] > 0 and q != "sync":
                            eng.wait_ge(sems[q], self.totals[q])
                    if self.totals["sync"] > 0:
                        eng.wait_ge(sems["sync"], self.totals["sync"])
            return body

        blk.tensor(run_engine("tensor"))
        blk.vector(run_engine("vector"))
        blk.scalar(run_engine("scalar"))
        blk.gpsimd(run_engine("gpsimd"))
        blk.sync(run_engine("sync"))


def chain_tail(P, cx, queue, src_ap, chain, f32=False):
    dst = cx.dummyf if f32 else cx.dummy
    P.dma(queue, lambda e, a=src_ap, d=dst: e.dma_start(d[:, 0:32], a),
          r=(), w=(), chain=chain)


def tiles_lhsT(w):
    Din, Dout = w.shape
    Ki, Mo = Din // 128, Dout // 128
    return np.ascontiguousarray(
        w.reshape(Ki, 128, Mo, 128).transpose(2, 1, 0, 3))


def perpart(vec):
    Mo = vec.shape[0] // 128
    return np.ascontiguousarray(vec.reshape(Mo, 128).T)


def make_consts():
    t = np.arange(128)
    ang = 2.0 * np.pi * np.outer(t, t) / 128.0
    CM = np.cos(ang).astype(np.float32)
    SM = (-np.sin(ang)).astype(np.float32)
    mask4 = np.zeros((4, 128, 512), np.float32)
    for p in range(4):
        tk = np.arange(128)[:, None]
        tq = np.arange(512)[None, :]
        mask4[p] = np.where(tq >= 128 * p + tk, 0.0, -1e9).astype(np.float32)
    ident = np.eye(128, dtype=np.float32)
    return CM, SM, mask4, ident


def prep_host(inputs):
    CM, SM, mask4, ident = make_consts()
    sh = {}
    sh["wq_t"] = tiles_lhsT(inputs["L0_attn_w_q"])
    sh["wk_t"] = tiles_lhsT(inputs["L0_attn_w_k"])
    sh["wv_r"] = np.ascontiguousarray(
        inputs["L0_attn_w_v"].reshape(NKI, 128, D))
    sh["wo_t"] = tiles_lhsT(inputs["L0_attn_w_o"])
    sh["bq_p"] = perpart(inputs["L0_attn_b_q"])
    sh["bk_p"] = perpart(inputs["L0_attn_b_k"])
    sh["bo_p"] = perpart(inputs["L0_attn_b_o"])
    sh["bv_p"] = perpart(inputs["L0_attn_b_v"])
    sh["wg_t"] = tiles_lhsT(inputs["L0_ffn_w_gate"])
    sh["wu_t"] = tiles_lhsT(inputs["L0_ffn_w_up"])
    sh["wd_t"] = tiles_lhsT(inputs["L0_ffn_w_down"])
    sh["sg_p"] = perpart(inputs["L0_ffn_s_gate"])
    sh["su_p"] = perpart(inputs["L0_ffn_s_up"])
    sh["sd_p"] = perpart(inputs["L0_ffn_s_down"])
    sh["l1wq_t"] = tiles_lhsT(np.ascontiguousarray(inputs["L1_attn_W_q"].T))
    sh["l1wkv_t"] = tiles_lhsT(np.ascontiguousarray(inputs["L1_attn_W_kv"].T))
    sh["l1wo_t"] = tiles_lhsT(np.ascontiguousarray(inputs["L1_attn_W_o"].T))
    cls = inputs["L1_ffn_classifier_w"]
    sh["cls_r"] = np.ascontiguousarray(
        cls.reshape(NEXP, NKI, 128).transpose(2, 1, 0))
    sh["ebias_t"] = np.ascontiguousarray(
        np.broadcast_to(inputs["L1_ffn_e_score_correction_bias"][None, :],
                        (128, NEXP)))
    gw = inputs["L1_ffn_gate_w"]
    uw = inputs["L1_ffn_up_w"]
    dw = inputs["L1_ffn_down_w"]
    sh["gw_t"] = np.stack([tiles_lhsT(np.ascontiguousarray(gw[e].T))
                           for e in range(NE)])
    sh["uw_t"] = np.stack([tiles_lhsT(np.ascontiguousarray(uw[e].T))
                           for e in range(NE)])
    sh["dw_t"] = np.stack([tiles_lhsT(np.ascontiguousarray(dw[e].T))
                           for e in range(NE)])
    sh["CM"] = CM
    sh["SM"] = SM
    sh["CM2"] = CM / 128.0
    sh["SM2"] = SM / 128.0
    sh["mask4"] = np.ascontiguousarray(mask4.transpose(1, 0, 2).reshape(128, 4 * 512))
    sh["ident"] = ident

    x = inputs["L0_attn_query"]
    percore = []
    for c in range(8):
        xc = x[2 * c:2 * c + 2].reshape(NT, D)
        percore.append({"xT": np.ascontiguousarray(xc.T)})
    return sh, percore


SHARED_SPECS = {
    "wq_t": ([16, 128, 16, 128], F32), "wk_t": ([16, 128, 16, 128], F32),
    "wv_r": ([16, 128, D], F32), "wo_t": ([16, 128, 16, 128], F32),
    "bq_p": ([128, 16], F32), "bk_p": ([128, 16], F32),
    "bo_p": ([128, 16], F32), "bv_p": ([128, 16], F32),
    "wg_t": ([64, 128, 16, 128], I8), "wu_t": ([64, 128, 16, 128], I8),
    "wd_t": ([16, 128, 64, 128], I8),
    "sg_p": ([128, 64], F32), "su_p": ([128, 64], F32), "sd_p": ([128, 16], F32),
    "l1wq_t": ([16, 128, 16, 128], F32), "l1wkv_t": ([16, 128, 16, 128], F32),
    "l1wo_t": ([16, 128, 16, 128], F32),
    "cls_r": ([128, NKI * NEXP], F32), "ebias_t": ([128, NEXP], F32),
    "gw_t": ([NE, 8, 128, 16, 128], F32), "uw_t": ([NE, 8, 128, 16, 128], F32),
    "dw_t": ([NE, 16, 128, 8, 128], F32),
    "CM": ([128, 128], F32), "SM": ([128, 128], F32),
    "CM2": ([128, 128], F32), "SM2": ([128, 128], F32),
    "mask4": ([128, 4 * 512], F32), "ident": ([128, 128], F32),
}
SCRATCH_SPECS = {
    "q_s": ([D, NT], BF16), "k_s": ([D, NT], BF16),
    "v_s": ([NT // 128, 128, D], BF16),
    "attn_s": ([D, NT], BF16),
    "y1_s": ([D, NT], BF16),
    "act_s": ([INTER, NT], BF16),
    "y2_s": ([D, NT], BF16),
    "q2_s": ([D, NT], BF16), "kv2_s": ([D, NT], BF16),
    "vh_s": ([D, NT], BF16),
    "y3_s": ([D, NT], BF16),
    "wts_s": ([NEXP, NT], F32),
    "wq_b": ([16, 128, 16, 128], BF16), "wk_b": ([16, 128, 16, 128], BF16),
    "wv_b": ([16, 128, D], BF16), "wo_b": ([16, 128, 16, 128], BF16),
    "l1wq_b": ([16, 128, 16, 128], BF16), "l1wkv_b": ([16, 128, 16, 128], BF16),
    "l1wo_b": ([16, 128, 16, 128], BF16),
    "gw_b": ([NE, 8, 128, 16, 128], BF16), "uw_b": ([NE, 8, 128, 16, 128], BF16),
    "dw_b": ([NE, 16, 128, 8, 128], BF16),
    "wd_b": ([16, 128, 64, 128], BF16),
    "x_b": ([D, NT], BF16),
}


class Ctx:
    pass


def setup_program(stack, phase_names, external_scratch):
    nc = bass.Bass("TRN2", target_bir_lowering=False)
    cx = Ctx()
    cx.nc = nc
    cx.io = {}
    cx.io["xT"] = nc.dram_tensor("xT", [D, NT], F32, kind="ExternalInput").ap()
    for nm, (shape, dt) in SHARED_SPECS.items():
        cx.io[nm] = nc.dram_tensor(nm, shape, dt, kind="ExternalInput").ap()
    reads, writes = set(), set()
    for ph in phase_names:
        r, w = PHASE_IO[ph]
        reads |= set(r)
        writes |= set(w)
    for nm, (shape, dt) in SCRATCH_SPECS.items():
        if nm.endswith("_b"):
            cx.io[nm] = nc.dram_tensor(nm, shape, dt, kind="Internal").ap()
        elif nm in external_scratch:
            if nm in writes:
                cx.io[nm] = nc.dram_tensor(nm, shape, dt,
                                           kind="ExternalOutput").ap()
            elif nm in reads:
                cx.io[nm] = nc.dram_tensor(nm, shape, dt,
                                           kind="ExternalInput").ap()
        elif (nm in reads) or (nm in writes):
            cx.io[nm] = nc.dram_tensor(nm, shape, dt, kind="Internal").ap()
    if "out" in writes:
        cx.io["out"] = nc.dram_tensor("out", [D, NT], F32,
                                      kind="ExternalOutput").ap()
    cx.psD = [stack.enter_context(nc.psum_tensor(f"psd{i}", [128, 1024], F32))
              for i in range(2)]
    cx.ps = [cx.psD[0][:, 0:512], cx.psD[0][:, 512:1024],
             cx.psD[1][:, 0:512], cx.psD[1][:, 512:1024]]
    cx.ps += [stack.enter_context(nc.psum_tensor(f"psb{i}", [128, 512], F32))[:, :]
              for i in range(4, 8)]
    sb = lambda name, shape, dt: stack.enter_context(
        nc.sbuf_tensor(name, shape, dt))
    cx.ones_col = sb("ones_col", [128, 1], BF16)
    cx.ones_row = sb("ones_row", [1, 128], BF16)
    cx.masks = sb("masks", [128, 4 * 512], BF16)
    cx.CM = sb("cCM", [128, 128], BF16)
    cx.SM = sb("cSM", [128, 128], BF16)
    cx.CM2 = sb("cCM2", [128, 128], BF16)
    cx.SM2 = sb("cSM2", [128, 128], BF16)
    cx.ident = sb("cident", [128, 128], F32)
    cx.dummy = sb("cdummy", [128, 32], BF16)
    cx.dummyf = sb("cdummyf", [128, 32], F32)
    cx.identb = sb("cidentb", [128, 128], BF16)
    cx.bq = sb("cbq", [128, 16], F32)
    cx.bk = sb("cbk", [128, 16], F32)
    cx.bo = sb("cbo", [128, 16], F32)
    cx.bvb = sb("cbvb", [128, 16], BF16)
    cx.sg = sb("csg", [128, 64], F32)
    cx.su = sb("csu", [128, 64], F32)
    cx.sd = sb("csd", [128, 16], F32)
    cx.cls = sb("ccls", [128, NKI * NEXP], BF16)
    cx.ebias = sb("cebias", [128, NEXP], F32)
    cx.sb = sb
    return nc, cx


def emit_init(P, cx):
    io = cx.io

    def cast_load(buf, src, key):
        P.dma("gpsimd", lambda e, b=buf, s=src: e.dma_start(b[:, :], s),
              r=(), w=(key,))

    def plain_load(buf, src, key):
        P.dma("sync", lambda e, b=buf, s=src: e.dma_start(b[:, :], s),
              r=(), w=(key,))

    P.v(lambda e: e.memset(cx.ones_col[:, :], 1.0), w=("c:ones_col",))
    P.v(lambda e: e.memset(cx.ones_row[:, :], 1.0), w=("c:ones_row",))
    cast_load(cx.masks, io["mask4"], "c:masks")
    cast_load(cx.CM, io["CM"], "c:CM")
    cast_load(cx.SM, io["SM"], "c:SM")
    cast_load(cx.CM2, io["CM2"], "c:CM2")
    cast_load(cx.SM2, io["SM2"], "c:SM2")
    cast_load(cx.bvb, io["bv_p"], "c:bvb")
    cast_load(cx.cls, io["cls_r"], "c:cls")
    cast_load(cx.identb, io["ident"], "c:identb")
    plain_load(cx.ident, io["ident"], "c:ident")
    convs = [("xT", "x_b", 16, None), ("wq_t", "wq_b", 16, None),
             ("wk_t", "wk_b", 16, None), ("wv_r", "wv_b", 8, None),
             ("wo_t", "wo_b", 16, None), ("wd_t", "wd_b", 16, None),
             ("l1wq_t", "l1wq_b", 16, None), ("l1wkv_t", "l1wkv_b", 16, None),
             ("l1wo_t", "l1wo_b", 16, None),
             ("gw_t", "gw_b", 16, "e a p k j -> (e a) p k j"),
             ("uw_t", "uw_b", 16, "e a p k j -> (e a) p k j"),
             ("dw_t", "dw_b", 8, None)]
    prev_key = None
    for src, dst, nch, rr in convs:
        key = f"wc:{'xb_s' if dst == 'x_b' else dst}"
        sap = io[src].rearrange(rr) if rr else io[src]
        dap = io[dst].rearrange(rr) if rr else io[dst]
        step = sap.shape[0] // nch
        for ci in range(nch):
            lo, hi = ci * step, (ci + 1) * step
            wkeys = (key, prev_key) if (ci == 0 and prev_key) else (key,)
            P.dma("gpsimd", lambda e, a=sap[lo:hi], b=dap[lo:hi]:
                  e.dma_start(b, a), r=(), w=wkeys)
        prev_key = key
    plain_load(cx.bq, io["bq_p"], "c:bq")
    plain_load(cx.bk, io["bk_p"], "c:bk")
    plain_load(cx.bo, io["bo_p"], "c:bo")
    plain_load(cx.sg, io["sg_p"], "c:sg")
    plain_load(cx.su, io["su_p"], "c:su")
    plain_load(cx.sd, io["sd_p"], "c:sd")
    plain_load(cx.ebias, io["ebias_t"], "c:ebias")


def emit_phaseA(P, cx, stack):
    nc, io = cx.nc, cx.io
    xslb = [stack.enter_context(
        nc.sbuf_tensor(f"A_xsl{i}", [128, NKI * 512], BF16)) for i in range(3)]
    ws = [stack.enter_context(nc.sbuf_tensor(f"A_ws{i}", [128, NKI * 128], BF16))
          for i in range(3)]
    wv = [stack.enter_context(nc.sbuf_tensor(f"A_wv{i}", [128, 1024], BF16))
          for i in range(4)]
    ob = [stack.enter_context(nc.sbuf_tensor(f"A_ob{i}", [128, 512], BF16))
          for i in range(4)]
    xT_r = io["xT"].rearrange("(k p) t -> p k t", p=128)

    for tg in range(8):
        c0 = tg * 512
        xs = tg % 3
        xsl = xslb[xs]
        for q in range(4):
            P.dma("gpsimd",
                  lambda e, q=q, c0=c0, xsl=xsl: e.dma_start(
                      xsl[:, :].rearrange("p (k t) -> p k t", t=512)[:, 4*q:4*q+4, :],
                      xT_r[:, 4*q:4*q+4, c0:c0 + 512]),
                  r=(), w=(f"A:xsl{xs}{q}",), chain=f"A:xsl{tg}")
        chain_tail(P, cx, "sync", xsl[:, 0:32], f"A:xsl{tg}")
        xkeys = tuple(f"A:xsl{xs}{q}" for q in range(4))

        for wi, (wt, wck, bias, dst) in enumerate(
                ((io["wq_b"], "wc:wq_b", cx.bq, io["q_s"]),
                 (io["wk_b"], "wc:wk_b", cx.bk, io["k_s"]))):
            for mo in range(16):
                sl = (tg * 32 + wi * 16 + mo) % 3
                P.dma("scalar",
                      lambda e, wt=wt, mo=mo, sl=sl: e.dma_start(
                          ws[sl][:, :].rearrange("p (k j) -> p k j", j=128),
                          wt[mo]),
                      r=(wck,), w=(f"A:ws{sl}",), chain="A:ws")
                pk = 4 + (mo % 2)
                for ki in range(16):
                    P.t(lambda e, ki=ki, sl=sl, pk=pk, xsl=xsl: e.matmul(
                        cx.ps[pk][:, :], ws[sl][:, ki*128:(ki+1)*128],
                        xsl[:, ki*512:(ki+1)*512],
                        start=(ki == 0), stop=(ki == 15)),
                        r=(f"A:ws{sl}",) + xkeys, w=(f"ps{pk}",))
                osl = mo % 4
                P.v(lambda e, pk=pk, osl=osl, bias=bias, wi=wi, mo=mo:
                    e.tensor_scalar(ob[osl][:, :], cx.ps[pk][:, :],
                                    bias[:, mo:mo+1], None, ALU.add),
                    r=(f"ps{pk}", "c:bq", "c:bk"), w=(f"A:ob{osl}",))
                P.dma("gpsimd",
                      lambda e, dst=dst, mo=mo, c0=c0, osl=osl: e.dma_start(
                          dst[mo*128:(mo+1)*128, c0:c0+512], ob[osl][:, :]),
                      r=(f"A:ob{osl}",), w=())

        for half in range(2):
            for ki in range(16):
                sl = (half * 16 + ki) % 4
                P.dma("sync",
                      lambda e, ki=ki, half=half, sl=sl: e.dma_start(
                          wv[sl][:, :],
                          io["wv_b"][ki, :, half*1024:(half+1)*1024]),
                      r=("wc:wv_b",), w=(f"A:wv{sl}",), chain="A:wv")
                for tc in range(4):
                    for ng in range(2):
                        pk = tc * 2 + ng
                        P.t(lambda e, ki=ki, tc=tc, ng=ng, sl=sl, pk=pk,
                            xsl=xsl: e.matmul(
                                cx.ps[pk][:, :],
                                xsl[:, ki*512+tc*128:ki*512+(tc+1)*128],
                                wv[sl][:, ng*512:(ng+1)*512],
                                start=(ki == 0), stop=(ki == 15)),
                            r=(f"A:wv{sl}",) + xkeys, w=(f"ps{pk}",))
            for tc in range(4):
                tch = tg * 4 + tc
                for ng in range(2):
                    pk = tc * 2 + ng
                    osl = pk % 4
                    P.v(lambda e, pk=pk, osl=osl: e.tensor_copy(
                        ob[osl][:, :], cx.ps[pk][:, :]),
                        r=(f"ps{pk}",), w=(f"A:ob{osl}",))
                    dv0 = half * 1024 + ng * 512
                    P.dma("gpsimd",
                          lambda e, tch=tch, dv0=dv0, osl=osl: e.dma_start(
                              io["v_s"][tch, :, dv0:dv0+512], ob[osl][:, :]),
                          r=(f"A:ob{osl}",), w=())


def emit_phaseB(P, cx, stack):
    nc, io = cx.nc, cx.io
    sb = lambda n, s_, d: stack.enter_context(nc.sbuf_tensor(n, s_, d))
    qh = [sb(f"B_qh{i}", [128, T], BF16) for i in range(3)]
    kh = [sb(f"B_kh{i}", [128, T], BF16) for i in range(3)]
    vb = [sb(f"B_vb{i}", [128, T], BF16) for i in range(3)]
    sq = [sb(f"B_sq{i}", [128, T], BF16) for i in range(2)]
    nbuf = [sb(f"B_nbuf{i}", [1, 2 * T], F32) for i in range(2)]
    grow = sb("B_grow", [1, 64], F32)
    cells = sb("B_cells", [1, 8], F32)
    cellb = sb("B_cellb", [1, 1], BF16)
    negbc = sb("B_negbc", [128, 1], F32)
    eb = [sb(f"B_eb{i}", [128, 1024], BF16) for i in range(2)]
    rec = [sb(f"B_rec{i}", [1, 512], F32) for i in range(2)]
    recb = [sb(f"B_recb{i}", [1, 512], BF16) for i in range(2)]
    sbb = [sb(f"B_sbb{i}", [128, 512], F32) for i in range(2)]
    ab = [sb(f"B_ab{i}", [128, 512], BF16) for i in range(2)]
    pairs = [(b, h) for b in range(NB) for h in range(H)]

    def emit_loads(p, qk_only=False):
        b, h = pairs[p]
        r0, c0, sl = h * 128, b * T, p % 3
        P.dma("sync", lambda e, r0=r0, c0=c0, sl=sl: e.dma_start(
            qh[sl][:, :], io["q_s"][r0:r0+128, c0:c0+T]), w=(f"B:qh{sl}",))
        P.dma("sync", lambda e, r0=r0, c0=c0, sl=sl: e.dma_start(
            kh[sl][:, :], io["k_s"][r0:r0+128, c0:c0+T]), w=(f"B:kh{sl}",))
        if qk_only:
            return
        P.dma("scalar", lambda e, b=b, h=h, sl=sl: e.dma_start(
            vb[sl][:, :].rearrange("p (ti j) -> p ti j", j=128),
            io["v_s"][b*16:(b+1)*16, :, h*128:(h+1)*128].rearrange(
                "ti p j -> p ti j")),
            w=(f"B:vb{sl}",))

    emit_loads(0, qk_only=True)
    for p in range(len(pairs)):
        sl, l2 = p % 3, p % 2
        if p + 1 < len(pairs):
            emit_loads(p + 1, qk_only=True)
        for src, key, off in ((qh[sl], f"B:qh{sl}", 0),
                              (kh[sl], f"B:kh{sl}", T)):
            P.v(lambda e, src=src, l2=l2: e.tensor_mul(
                sq[l2][:, :], src[:, :], src[:, :]),
                r=(key,), w=(f"B:sq{l2}",))
            for g4 in range(4):
                pk = 6 + g4 % 2
                P.t(lambda e, g4=g4, l2=l2, pk=pk: e.matmul(
                    cx.ps[pk][0:1, :], cx.ones_col[:, :],
                    sq[l2][:, g4*512:(g4+1)*512], start=True, stop=True),
                    r=(f"B:sq{l2}", "c:ones_col"), w=(f"ps{pk}",))
                P.v(lambda e, g4=g4, off=off, l2=l2, pk=pk: e.tensor_copy(
                    nbuf[l2][0:1, off+g4*512:off+(g4+1)*512],
                    cx.ps[pk][0:1, :]),
                    r=(f"ps{pk}",), w=(f"B:nbuf{l2}",))
        P.v(lambda e, l2=l2, p=p: e.reduce_max(grow[0:1, p:p+1],
                                               nbuf[l2][0:1, 0:T], AX.X),
            r=(f"B:nbuf{l2}",), w=("B:grow",))
        P.v(lambda e, l2=l2, p=p: e.reduce_max(grow[0:1, 32+p:33+p],
                                               nbuf[l2][0:1, T:2*T], AX.X),
            r=(f"B:nbuf{l2}",), w=("B:grow",))
    P.v(lambda e: e.tensor_copy(sq[0][:, :], sq[0][:, :]), w=("B:pad",))
    P.v(lambda e: e.tensor_copy(sq[1][:, :], sq[1][:, :]), w=("B:pad",))
    P.v(lambda e: e.reduce_max(cells[0:1, 0:1], grow[0:1, 0:32], AX.X),
        r=("B:grow", "B:pad"), w=("B:c0",))
    P.v(lambda e: e.reduce_max(cells[0:1, 1:2], grow[0:1, 32:64], AX.X),
        r=("B:grow",), w=("B:c1",))
    P.g(lambda e: e.tensor_add(cells[0:1, 2:3], cells[0:1, 0:1],
                               cells[0:1, 1:2]),
        r=("B:c0", "B:c1"), w=("B:c2",))
    P.v(lambda e: e.tensor_scalar_mul(cells[0:1, 3:4], cells[0:1, 2:3],
                                      -SCALE_ATT / 2.0),
        r=("B:c2",), w=("B:c3",))
    P.g(lambda e: e.tensor_copy(cellb[0:1, 0:1], cells[0:1, 3:4]),
        r=("B:c3",), w=("B:cellb",))
    P.t(lambda e: e.matmul(cx.ps[6][:, 0:1], cx.ones_row[:, :],
                           cellb[0:1, 0:1], start=True, stop=True),
        r=("B:cellb", "c:ones_row"), w=("ps6",))
    P.v(lambda e: e.tensor_copy(negbc[:, 0:1], cx.ps[6][:, 0:1]),
        r=("ps6",), w=("B:negbc",))

    def emit_main(p):
        b, h = pairs[p]
        r0, c0, sl = h * 128, b * T, p % 3

        def s_duo(g, q, nduo):
            dl = q % 2
            pan = "ps01" if dl == 0 else "ps23"
            dq = q - (nduo - 2)
            for i in range(2):
                j = 2 * q + i
                diag = dq >= 0
                P.t(lambda e, j=j, g=g, dl=dl, i=i, sl=sl, diag=diag:
                    e.matmul(
                        cx.psD[dl][:, i*512:(i+1)*512],
                        kh[sl][:, j*128:(j+1)*128],
                        qh[sl][:, g*512:(g+1)*512], start=True,
                        stop=not diag),
                    r=(f"B:kh{sl}", f"B:qh{sl}"), w=(pan,))
                if diag:
                    pat = dq * 2 + i
                    P.t(lambda e, dl=dl, i=i, pat=pat: e.matmul(
                        cx.psD[dl][:, i*512:(i+1)*512], cx.identb[:, :],
                        cx.masks[:, pat*512:(pat+1)*512],
                        start=False, stop=True),
                        r=("c:identb", "c:masks"), w=(pan,))

        def exp_duo(g, q, nduo):
            dl = q % 2
            pan = "ps01" if dl == 0 else "ps23"
            P.s(lambda e, dl=dl: e.activation(
                eb[dl][:, :], cx.psD[dl][:, :], AF.Exp,
                bias=negbc[:, 0:1], scale=SCALE_ATT),
                r=(pan, "B:negbc"), w=(f"B:eb{dl}",))

        def denpv_duo(g, q, nduo):
            dl = q % 2
            gp = g % 2
            pv, dn = 4 + gp, 6 + gp
            for i in range(2):
                j = 2 * q + i
                P.t(lambda e, dl=dl, i=i, q=q, nduo=nduo, dn=dn: e.matmul(
                    cx.ps[dn][0:1, :], cx.ones_col[:, :],
                    eb[dl][:, i*512:(i+1)*512],
                    start=(q == 0 and i == 0),
                    stop=(q == nduo - 1 and i == 1)),
                    r=(f"B:eb{dl}", "c:ones_col"), w=(f"ps{dn}",))
                P.t(lambda e, dl=dl, i=i, j=j, q=q, nduo=nduo, sl=sl, pv=pv:
                    e.matmul(
                        cx.ps[pv][:, :], vb[sl][:, j*128:(j+1)*128],
                        eb[dl][:, i*512:(i+1)*512],
                        start=(q == 0 and i == 0),
                        stop=(q == nduo - 1 and i == 1)),
                    r=(f"B:eb{dl}", f"B:vb{sl}"), w=(f"ps{pv}",))

        for g in range(4):
            nduo = 2 * g + 2
            gp = g % 2
            pv, dn = 4 + gp, 6 + gp
            s_duo(g, 0, nduo)
            for q in range(nduo):
                exp_duo(g, q, nduo)
                if q + 1 < nduo:
                    s_duo(g, q + 1, nduo)
                denpv_duo(g, q, nduo)
            P.v(lambda e, gp=gp, dn=dn: e.reciprocal(rec[gp][0:1, :],
                                                     cx.ps[dn][0:1, :]),
                r=(f"ps{dn}",), w=(f"B:rec{gp}",))
            P.v(lambda e, gp=gp: e.tensor_copy(recb[gp][0:1, :],
                                               rec[gp][0:1, :]),
                r=(f"B:rec{gp}",), w=(f"B:recb{gp}",))
            P.t(lambda e, gp=gp, dn=dn: e.matmul(
                cx.ps[dn][:, :], cx.ones_row[:, :], recb[gp][0:1, :],
                start=True, stop=True),
                r=(f"B:recb{gp}", "c:ones_row"), w=(f"ps{dn}",))
            P.v(lambda e, gp=gp, dn=dn: e.tensor_copy(sbb[gp][:, :],
                                                      cx.ps[dn][:, :]),
                r=(f"ps{dn}",), w=(f"B:sbb{gp}",))
            P.v(lambda e, gp=gp, pv=pv: e.tensor_mul(
                ab[gp][:, :], cx.ps[pv][:, :], sbb[gp][:, :]),
                r=(f"ps{pv}", f"B:sbb{gp}"), w=(f"B:ab{gp}",))
            P.dma("sync", lambda e, r0=r0, c0=c0, g=g, gp=gp: e.dma_start(
                io["attn_s"][r0:r0+128, c0+g*512:c0+(g+1)*512],
                ab[gp][:, :]),
                r=(f"B:ab{gp}",), w=())

    emit_loads(0)
    emit_loads(1)
    for p in range(len(pairs)):
        if p + 2 < len(pairs):
            emit_loads(p + 2)
        emit_main(p)


class LNKit:
    def __init__(self, P, cx, stack, pfx, out_f32=False):
        nc = cx.nc
        self.P, self.cx, self.pfx = P, cx, pfx
        sb = lambda n, s, d: stack.enter_context(nc.sbuf_tensor(pfx + n, s, d))
        self.xr = sb("_xr", [128, NKI * 512], F32)
        self.xrb = sb("_xrb", [128, NKI * 512], BF16)
        self.sqb = sb("_sqb", [128, NKI * 512], BF16)
        self.yb = sb("_yb", [128, NKI * 512], F32 if out_f32 else BF16)
        row = lambda n: sb(n, [1, 512], F32)
        self.mbuf, self.e2buf, self.m2buf = row("_m"), row("_e2"), row("_m2")
        self.varbuf, self.sdbuf = row("_var"), row("_sd")
        self.abuf, self.bbuf = row("_a"), row("_b")
        self.abb = sb("_abb", [1, 512], BF16)
        self.bbb = sb("_bbb", [1, 512], BF16)
        self.out_f32 = out_f32

    def emit(self, c0, dst_r):
        P, cx, pfx = self.P, self.cx, self.pfx
        xr, xrb, sqb, yb = self.xr, self.xrb, self.sqb, self.yb
        for mo in range(16):
            P.v(lambda e, mo=mo: e.tensor_copy(
                xrb[:, mo*512:(mo+1)*512], xr[:, mo*512:(mo+1)*512]),
                r=(f"{pfx}:xr{mo}",), w=(f"{pfx}:xrb{mo}",))
            P.s(lambda e, mo=mo: e.activation(
                sqb[:, mo*512:(mo+1)*512], xr[:, mo*512:(mo+1)*512],
                AF.Square),
                r=(f"{pfx}:xr{mo}",), w=(f"{pfx}:sqb{mo}",))
        for mo in range(16):
            P.t(lambda e, mo=mo: e.matmul(
                cx.ps[6][0:1, :], cx.ones_col[:, :],
                xrb[:, mo*512:(mo+1)*512],
                start=(mo == 0), stop=(mo == 15)),
                r=(f"{pfx}:xrb{mo}", "c:ones_col"), w=("ps6",))
            P.t(lambda e, mo=mo: e.matmul(
                cx.ps[7][0:1, :], cx.ones_col[:, :],
                sqb[:, mo*512:(mo+1)*512],
                start=(mo == 0), stop=(mo == 15)),
                r=(f"{pfx}:sqb{mo}", "c:ones_col"), w=("ps7",))
        P.v(lambda e: e.tensor_scalar_mul(self.mbuf[0:1, :], cx.ps[6][0:1, :],
                                          1.0 / D),
            r=("ps6",), w=(f"{pfx}:m",))
        P.v(lambda e: e.tensor_scalar_mul(self.e2buf[0:1, :], cx.ps[7][0:1, :],
                                          1.0 / D),
            r=("ps7",), w=(f"{pfx}:e2",))
        P.v(lambda e: e.tensor_mul(self.m2buf[0:1, :], self.mbuf[0:1, :],
                                   self.mbuf[0:1, :]),
            r=(f"{pfx}:m",), w=(f"{pfx}:m2",))
        P.v(lambda e: e.tensor_sub(self.varbuf[0:1, :], self.e2buf[0:1, :],
                                   self.m2buf[0:1, :]),
            r=(f"{pfx}:e2", f"{pfx}:m2"), w=(f"{pfx}:var",))
        P.v(lambda e: e.tensor_scalar_add(self.varbuf[0:1, :],
                                          self.varbuf[0:1, :], 1e-5),
            r=(f"{pfx}:var",), w=(f"{pfx}:var",))
        P.s(lambda e: e.activation(self.sdbuf[0:1, :], self.varbuf[0:1, :],
                                   AF.Sqrt),
            r=(f"{pfx}:var",), w=(f"{pfx}:sd",))
        P.v(lambda e: e.reciprocal(self.abuf[0:1, :], self.sdbuf[0:1, :]),
            r=(f"{pfx}:sd",), w=(f"{pfx}:a",))
        P.v(lambda e: e.scalar_tensor_tensor(self.bbuf[0:1, :],
                                             self.mbuf[0:1, :], -1.0,
                                             self.abuf[0:1, :],
                                             ALU.mult, ALU.mult),
            r=(f"{pfx}:m", f"{pfx}:a"), w=(f"{pfx}:b",))
        P.v(lambda e: e.tensor_copy(self.abb[0:1, :], self.abuf[0:1, :]),
            r=(f"{pfx}:a",), w=(f"{pfx}:abb",))
        P.v(lambda e: e.tensor_copy(self.bbb[0:1, :], self.bbuf[0:1, :]),
            r=(f"{pfx}:b",), w=(f"{pfx}:bbb",))
        P.t(lambda e: e.matmul(cx.ps[4][:, :], cx.ones_row[:, :],
                               self.abb[0:1, :], start=True, stop=True),
            r=(f"{pfx}:abb", "c:ones_row"), w=("ps4",))
        P.t(lambda e: e.matmul(cx.ps[5][:, :], cx.ones_row[:, :],
                               self.bbb[0:1, :], start=True, stop=True),
            r=(f"{pfx}:bbb", "c:ones_row"), w=("ps5",))
        for mo in range(16):
            P.v(lambda e, mo=mo: e.tensor_mul(
                xrb[:, mo*512:(mo+1)*512], xr[:, mo*512:(mo+1)*512],
                cx.ps[4][:, :]),
                r=(f"{pfx}:xr{mo}", "ps4"), w=(f"{pfx}:xrb{mo}",))
            P.v(lambda e, mo=mo: e.tensor_add(
                yb[:, mo*512:(mo+1)*512], xrb[:, mo*512:(mo+1)*512],
                cx.ps[5][:, :]),
                r=(f"{pfx}:xrb{mo}", "ps5"), w=(f"{pfx}:yb{mo}",))
        for q in range(4):
            P.dma("sync", lambda e, q=q, c0=c0: e.dma_start(
                dst_r[:, 4*q:4*q+4, c0:c0+512],
                yb[:, :].rearrange("p (k t) -> p k t", t=512)[:, 4*q:4*q+4, :]),
                r=tuple(f"{self.pfx}:yb{mo}" for mo in range(4*q, 4*q+4)), w=())


def emit_out_proj_ln(P, cx, stack, w_t, src, resid, dst, use_bias):
    w_b = {"wo_t": "wo_b", "l1wo_t": "l1wo_b"}[w_t]
    nc, io = cx.nc, cx.io
    pfx = dst
    sb = lambda n, s, d: stack.enter_context(nc.sbuf_tensor(pfx + n, s, d))
    asl = sb("_asl", [128, NKI * 512], BF16)
    rsl = sb("_rsl", [128, NKI * 512], F32)
    ws = [sb(f"_ws{i}", [128, NKI * 128], BF16) for i in range(3)]
    nw = [0]
    biasc = sb("_biasc", [128, 16], F32)
    ln = LNKit(P, cx, stack, pfx)
    xr = ln.xr
    wt = io[w_b]
    wck = f"wc:{w_b}"
    src_r = io[src].rearrange("(k p) t -> p k t", p=128)
    res_r = io[resid].rearrange("(k p) t -> p k t", p=128)
    dst_r = io[dst].rearrange("(k p) t -> p k t", p=128)

    if use_bias:
        for mo in range(16):
            sl = nw[0] % 3
            nw[0] += 1
            P.dma("scalar", lambda e, mo=mo, sl=sl: e.dma_start(
                ws[sl][:, :].rearrange("p (k j) -> p k j", j=128), wt[mo]),
                r=(wck,), w=(f"{pfx}:ws{sl}",), chain=f"{pfx}:ws")
            chain_tail(P, cx, "scalar", ws[sl][:, 0:32], f"{pfx}:ws")
            for ki in range(16):
                P.t(lambda e, ki=ki, sl=sl: e.matmul(
                    cx.ps[6][:, 0:1], ws[sl][:, ki*128:(ki+1)*128],
                    cx.bvb[:, ki:ki+1], start=(ki == 0), stop=(ki == 15)),
                    r=(f"{pfx}:ws{sl}", "c:bvb"), w=("ps6",))
            P.v(lambda e, mo=mo: e.tensor_scalar(
                biasc[:, mo:mo+1], cx.ps[6][:, 0:1], cx.bo[:, mo:mo+1],
                None, ALU.add),
                r=("ps6", "c:bo"), w=(f"{pfx}:biasc",))

    for tg in range(8):
        c0 = tg * 512
        for q in range(4):
            P.dma("sync", lambda e, q=q, c0=c0: e.dma_start(
                asl[:, :].rearrange("p (k t) -> p k t", t=512)[:, 4*q:4*q+4, :],
                src_r[:, 4*q:4*q+4, c0:c0+512]),
                w=(f"{pfx}:asl{q}",))
            P.dma("gpsimd", lambda e, q=q, c0=c0: e.dma_start(
                rsl[:, :].rearrange("p (k t) -> p k t", t=512)[:, 4*q:4*q+4, :],
                res_r[:, 4*q:4*q+4, c0:c0+512]),
                w=(f"{pfx}:rsl{q}",))
        akeys = tuple(f"{pfx}:asl{q}" for q in range(4))
        rkeys = tuple(f"{pfx}:rsl{q}" for q in range(4))
        for mo in range(16):
            sl = nw[0] % 3
            nw[0] += 1
            pk = mo % 2
            P.dma("scalar", lambda e, mo=mo, sl=sl: e.dma_start(
                ws[sl][:, :].rearrange("p (k j) -> p k j", j=128), wt[mo]),
                r=(wck,), w=(f"{pfx}:ws{sl}",), chain=f"{pfx}:ws")
            chain_tail(P, cx, "scalar", ws[sl][:, 0:32], f"{pfx}:ws")
            for ki in range(16):
                P.t(lambda e, ki=ki, sl=sl, pk=pk: e.matmul(
                    cx.ps[pk][:, :], ws[sl][:, ki*128:(ki+1)*128],
                    asl[:, ki*512:(ki+1)*512],
                    start=(ki == 0), stop=(ki == 15)),
                    r=(f"{pfx}:ws{sl}",) + akeys, w=(f"ps{pk}",))
            if use_bias:
                P.v(lambda e, mo=mo, pk=pk: e.scalar_tensor_tensor(
                    xr[:, mo*512:(mo+1)*512], cx.ps[pk][:, :],
                    biasc[:, mo:mo+1], rsl[:, mo*512:(mo+1)*512],
                    ALU.add, ALU.add),
                    r=(f"ps{pk}", f"{pfx}:biasc") + rkeys, w=(f"{pfx}:xr{mo}",))
            else:
                P.v(lambda e, mo=mo, pk=pk: e.tensor_add(
                    xr[:, mo*512:(mo+1)*512], cx.ps[pk][:, :],
                    rsl[:, mo*512:(mo+1)*512]),
                    r=(f"ps{pk}",) + rkeys, w=(f"{pfx}:xr{mo}",))
        ln.emit(c0, dst_r)


def emit_phaseD1(P, cx, stack):
    nc, io = cx.nc, cx.io
    sb = lambda n, s, d: stack.enter_context(nc.sbuf_tensor(n, s, d))
    ybig = sb("D1_ybig", [128, NKI * NT], BF16)
    wg = [sb(f"D1_wg{i}", [128, NKI * 128], BF16) for i in range(3)]
    wu = [sb(f"D1_wu{i}", [128, NKI * 128], BF16) for i in range(3)]
    sigb = sb("D1_sig", [128, 512], BF16)
    gbuf = sb("D1_g", [128, 512], F32)
    ubuf = sb("D1_u", [128, 512], BF16)
    gsb = sb("D1_gs", [128, 512], BF16)
    actb = [sb(f"D1_act{i}", [128, 512], BF16) for i in range(2)]
    y_r = io["y1_s"].rearrange("(k p) t -> p k t", p=128)
    a_r = io["act_s"].rearrange("(k p) t -> p k t", p=128)
    for ki in range(16):
        P.dma("sync", lambda e, ki=ki: e.dma_start(
            ybig[:, ki*NT:(ki+1)*NT], y_r[:, ki, :]), w=(f"D1:y{ki}",))
    ykeys = tuple(f"D1:y{ki}" for ki in range(16))
    for m in range(64):
        sl = m % 3
        P.dma("gpsimd", lambda e, m=m, sl=sl: e.dma_start(
            wg[sl][:, :].rearrange("p (k j) -> p k j", j=128), io["wg_t"][m]),
            w=(f"D1:wg{sl}",), chain="D1:wg")
        P.dma("gpsimd", lambda e, m=m, sl=sl: e.dma_start(
            wu[sl][:, :].rearrange("p (k j) -> p k j", j=128), io["wu_t"][m]),
            w=(f"D1:wu{sl}",), chain="D1:wu")
        for tg in range(8):
            pg = tg % 2
            pu = 2 + tg % 2
            for ki in range(16):
                P.t(lambda e, ki=ki, sl=sl, tg=tg, pg=pg: e.matmul(
                    cx.ps[pg][:, :], wg[sl][:, ki*128:(ki+1)*128],
                    ybig[:, ki*NT+tg*512:ki*NT+(tg+1)*512],
                    start=(ki == 0), stop=(ki == 15)),
                    r=(f"D1:wg{sl}",) + ykeys, w=(f"ps{pg}",))
            for ki in range(16):
                P.t(lambda e, ki=ki, sl=sl, tg=tg, pu=pu: e.matmul(
                    cx.ps[pu][:, :], wu[sl][:, ki*128:(ki+1)*128],
                    ybig[:, ki*NT+tg*512:ki*NT+(tg+1)*512],
                    start=(ki == 0), stop=(ki == 15)),
                    r=(f"D1:wu{sl}",) + ykeys, w=(f"ps{pu}",))
            P.s(lambda e, m=m, pg=pg: e.activation(
                sigb[:, :], cx.ps[pg][:, :], AF.Sigmoid,
                scale=cx.sg[:, m:m+1]),
                r=(f"ps{pg}", "c:sg"), w=("D1:sig",))
            P.v(lambda e, m=m, pg=pg: e.tensor_scalar_mul(
                gbuf[:, :], cx.ps[pg][:, :], cx.sg[:, m:m+1]),
                r=(f"ps{pg}", "c:sg"), w=("D1:g",))
            P.v(lambda e, m=m, pu=pu: e.tensor_scalar_mul(
                ubuf[:, :], cx.ps[pu][:, :], cx.su[:, m:m+1]),
                r=(f"ps{pu}", "c:su"), w=("D1:u",))
            P.v(lambda e: e.tensor_mul(gsb[:, :], gbuf[:, :], sigb[:, :]),
                r=("D1:g", "D1:sig"), w=("D1:gs",))
            asl_ = tg % 2
            P.v(lambda e, asl_=asl_: e.tensor_mul(
                actb[asl_][:, :], gsb[:, :], ubuf[:, :]),
                r=("D1:gs", "D1:u"), w=(f"D1:act{asl_}",))
            P.dma("sync", lambda e, m=m, tg=tg, asl_=asl_: e.dma_start(
                a_r[:, m, tg*512:(tg+1)*512], actb[asl_][:, :]),
                r=(f"D1:act{asl_}",), w=())


def emit_phaseD3(P, cx, stack):
    nc, io = cx.nc, cx.io
    sb = lambda n, s, d: stack.enter_context(nc.sbuf_tensor(n, s, d))
    abig = sb("D3_abig", [128, 64 * 512], BF16)
    rsl = sb("D3_rsl", [128, NKI * 512], BF16)
    wd = sb("D3_wd", [128, 3 * 32 * 128], BF16)
    ln = LNKit(P, cx, stack, "D3")
    xr = ln.xr
    a_r = io["act_s"].rearrange("(k p) t -> p k t", p=128)
    y_r = io["y1_s"].rearrange("(k p) t -> p k t", p=128)
    dst_r = io["y2_s"].rearrange("(k p) t -> p k t", p=128)
    for tg in range(8):
        c0 = tg * 512
        for q in range(8):
            P.dma("sync", lambda e, q=q, c0=c0: e.dma_start(
                abig[:, :].rearrange("p (k t) -> p k t", t=512)[:, 8*q:8*q+8, :],
                a_r[:, 8*q:8*q+8, c0:c0+512]),
                w=(f"D3:a{q}",))
        for q in range(4):
            P.dma("sync", lambda e, q=q, c0=c0: e.dma_start(
                rsl[:, :].rearrange("p (k t) -> p k t", t=512)[:, 4*q:4*q+4, :],
                y_r[:, 4*q:4*q+4, c0:c0+512]),
                w=(f"D3:r{q}",))
        akeys = tuple(f"D3:a{q}" for q in range(8))
        rkeys = tuple(f"D3:r{q}" for q in range(4))
        for mo in range(16):
            pk = mo % 2
            for hf in range(2):
                hs = (mo * 2 + hf) % 3
                P.dma("scalar",
                      lambda e, mo=mo, hf=hf, hs=hs: e.dma_start(
                          wd[:, hs*4096:(hs+1)*4096].rearrange(
                              "p (k j) -> p k j", j=128),
                          io["wd_b"][mo, :, hf*32:(hf+1)*32, :]),
                      r=("wc:wd_b",), w=(f"D3:wd{hs}",), chain=f"D3:wd{tg}")
            for ki in range(64):
                hf = ki // 32
                hs = (mo * 2 + hf) % 3
                lk = ki % 32
                P.t(lambda e, lk=lk, ki=ki, pk=pk, hs=hs: e.matmul(
                    cx.ps[pk][:, :], wd[:, hs*4096+lk*128:hs*4096+(lk+1)*128],
                    abig[:, ki*512:(ki+1)*512],
                    start=(ki == 0), stop=(ki == 63)),
                    r=(f"D3:wd{hs}",) + akeys, w=(f"ps{pk}",))
            P.v(lambda e, mo=mo, pk=pk: e.scalar_tensor_tensor(
                xr[:, mo*512:(mo+1)*512], cx.ps[pk][:, :],
                cx.sd[:, mo:mo+1], rsl[:, mo*512:(mo+1)*512],
                ALU.mult, ALU.add),
                r=(f"ps{pk}", "c:sd") + rkeys, w=(f"D3:xr{mo}",))
        chain_tail(P, cx, "scalar", wd[:, 0:32], f"D3:wd{tg}")
        ln.emit(c0, dst_r)


def emit_phaseE1(P, cx, stack):
    nc, io = cx.nc, cx.io
    sb = lambda n, s, d: stack.enter_context(nc.sbuf_tensor(n, s, d))
    ybig = sb("E1_ybig", [128, NKI * NT], BF16)
    ws = [sb(f"E1_ws{i}", [128, NKI * 128], BF16) for i in range(3)]
    ob = [sb(f"E1_ob{i}", [128, 512], BF16) for i in range(4)]
    y_r = io["y2_s"].rearrange("(k p) t -> p k t", p=128)
    for ki in range(16):
        P.dma("sync", lambda e, ki=ki: e.dma_start(
            ybig[:, ki*NT:(ki+1)*NT], y_r[:, ki, :]), w=(f"E1:y{ki}",))
    ykeys = tuple(f"E1:y{ki}" for ki in range(16))
    for wi, (wname, dname) in enumerate((("l1wq_b", "q2_s"),
                                         ("l1wkv_b", "kv2_s"))):
        wt = io[wname]
        wck = f"wc:{wname}"
        dst = io[dname]
        for mo in range(16):
            sl = (wi * 16 + mo) % 3
            P.dma("scalar", lambda e, wt=wt, mo=mo, sl=sl: e.dma_start(
                ws[sl][:, :].rearrange("p (k j) -> p k j", j=128), wt[mo]),
                r=(wck,), w=(f"E1:ws{sl}",), chain="E1:ws")
            for tg in range(8):
                pk = tg % 2
                for ki in range(16):
                    P.t(lambda e, ki=ki, sl=sl, tg=tg, pk=pk: e.matmul(
                        cx.ps[pk][:, :], ws[sl][:, ki*128:(ki+1)*128],
                        ybig[:, ki*NT+tg*512:ki*NT+(tg+1)*512],
                        start=(ki == 0), stop=(ki == 15)),
                        r=(f"E1:ws{sl}",) + ykeys, w=(f"ps{pk}",))
                osl = tg % 4
                P.v(lambda e, pk=pk, osl=osl: e.tensor_copy(
                    ob[osl][:, :], cx.ps[pk][:, :]),
                    r=(f"ps{pk}",), w=(f"E1:ob{osl}",))
                P.dma("sync",
                      lambda e, dst=dst, mo=mo, tg=tg, osl=osl: e.dma_start(
                          dst[mo*128:(mo+1)*128, tg*512:(tg+1)*512],
                          ob[osl][:, :]),
                      r=(f"E1:ob{osl}",), w=())


def emit_phaseE2(P, cx, stack):
    nc, io = cx.nc, cx.io
    sb = lambda n, s_, d: stack.enter_context(nc.sbuf_tensor(n, s_, d))
    qh = [sb(f"E2_qh{i}", [128, T], BF16) for i in range(3)]
    kvh = [sb(f"E2_kvh{i}", [128, T], BF16) for i in range(3)]
    sq2 = [sb(f"E2_sq{i}", [128, T], BF16) for i in range(2)]
    nrow = [sb(f"E2_nrow{i}", [1, T], F32) for i in range(2)]
    nrowb = [sb(f"E2_nrowb{i}", [1, T], BF16) for i in range(2)]
    sdrow = [sb(f"E2_sdrow{i}", [1, T], F32) for i in range(2)]
    qhat = [sb(f"E2_qhat{i}", [128, T], BF16) for i in range(2)]
    kcs = [sb(f"E2_kcs{i}", [128, T], BF16) for i in range(2)]
    khat = sb("E2_khat", [128, T], BF16)
    zerosb = sb("E2_zeros", [128, T], BF16)
    qfre = sb("E2_qfre", [128, T], BF16)
    qfim = sb("E2_qfim", [128, T], BF16)
    cfre = sb("E2_cfre", [128, T], BF16)
    cfim = sb("E2_cfim", [128, T], BF16)
    qvre = sb("E2_qvre", [128, T], BF16)
    qvim = sb("E2_qvim", [128, T], BF16)
    tm1 = sb("E2_tm1", [128, T], BF16)
    tm2 = sb("E2_tm2", [128, T], BF16)
    tm3 = sb("E2_tm3", [128, T], BF16)
    tm4 = sb("E2_tm4", [128, T], BF16)
    vhb = sb("E2_vhb", [128, T], BF16)
    bcb = [sb(f"E2_bcb{i}", [128, 512], BF16) for i in range(2)]
    P.v(lambda e: e.memset(zerosb[:, :], 0.0), w=("E2:zeros",))
    pairs = [(b, h) for b in range(NB) for h in range(H)]

    def emit_loads(p):
        b, h = pairs[p]
        r0, c0, sl = h * 128, b * T, p % 3
        P.dma("sync", lambda e, r0=r0, c0=c0, sl=sl: e.dma_start(
            qh[sl][:, :], io["q2_s"][r0:r0+128, c0:c0+T]), w=(f"E2:qh{sl}",))
        P.dma("sync", lambda e, r0=r0, c0=c0, sl=sl: e.dma_start(
            kvh[sl][:, :], io["kv2_s"][r0:r0+128, c0:c0+T]), w=(f"E2:kvh{sl}",))

    def stage_a(p):
        sl, l2 = p % 3, p % 2
        for src, skey, dstb, dkey in (
                (qh[sl], f"E2:qh{sl}", qhat[l2], f"E2:qhat{l2}"),
                (kvh[sl], f"E2:kvh{sl}", khat, "E2:khat")):
            P.v(lambda e, src=src, l2=l2: e.tensor_mul(
                sq2[l2][:, :], src[:, :], src[:, :]),
                r=(skey,), w=(f"E2:sq{l2}",))
            for g in range(4):
                P.t(lambda e, g=g, l2=l2: e.matmul(
                    cx.ps[5][0:1, :], cx.ones_col[:, :],
                    sq2[l2][:, g*512:(g+1)*512], start=True, stop=True),
                    r=(f"E2:sq{l2}", "c:ones_col"), w=("ps5",))
                P.s(lambda e, g=g, l2=l2: e.activation(
                    sdrow[l2][0:1, g*512:(g+1)*512], cx.ps[5][0:1, :],
                    AF.Sqrt, scale=128.0),
                    r=("ps5",), w=(f"E2:sd{l2}",))
            P.v(lambda e, l2=l2: e.reciprocal(nrow[l2][0:1, :],
                                              sdrow[l2][0:1, :]),
                r=(f"E2:sd{l2}",), w=(f"E2:nrow{l2}",))
            P.v(lambda e, l2=l2: e.tensor_copy(nrowb[l2][0:1, :],
                                               nrow[l2][0:1, :]),
                r=(f"E2:nrow{l2}",), w=(f"E2:nrowb{l2}",))
            for g in range(4):
                bs = g % 2
                P.t(lambda e, g=g, l2=l2: e.matmul(
                    cx.ps[6][:, :], cx.ones_row[:, :],
                    nrowb[l2][0:1, g*512:(g+1)*512], start=True, stop=True),
                    r=(f"E2:nrowb{l2}", "c:ones_row"), w=("ps6",))
                P.s(lambda e, bs=bs: e.copy(bcb[bs][:, :], cx.ps[6][:, :]),
                    r=("ps6",), w=(f"E2:bcb{bs}",))
                emit = P.v if g % 2 == 0 else P.g
                emit(lambda e, g=g, src=src, dstb=dstb, bs=bs: e.tensor_mul(
                    dstb[:, g*512:(g+1)*512], src[:, g*512:(g+1)*512],
                    bcb[bs][:, :]),
                    r=(skey, f"E2:bcb{bs}"), w=(dkey,))
        P.v(lambda e, l2=l2: e.tensor_tensor_scan(
            kcs[l2][:, :], khat[:, :], zerosb[:, :], 0.0, ALU.add, ALU.add),
            r=("E2:khat", "E2:zeros"), w=(f"E2:kcs{l2}",))

    def stage_b(p):
        b, h = pairs[p]
        r0, c0, l2 = h * 128, b * T, p % 2
        for g in range(4):
            gg = (g*512, (g+1)*512)
            for lh, lkey, srcb, skey, dstb, dkey, pk in (
                    (cx.CM, "c:CM", qhat[l2], f"E2:qhat{l2}", qfre, "E2:qfre", 0),
                    (cx.SM, "c:SM", qhat[l2], f"E2:qhat{l2}", qfim, "E2:qfim", 1),
                    (cx.CM, "c:CM", kcs[l2], f"E2:kcs{l2}", cfre, "E2:cfre", 2),
                    (cx.SM, "c:SM", kcs[l2], f"E2:kcs{l2}", cfim, "E2:cfim", 3)):
                P.t(lambda e, lh=lh, srcb=srcb, gg=gg, pk=pk: e.matmul(
                    cx.ps[pk][:, :], lh[:, :], srcb[:, gg[0]:gg[1]],
                    start=True, stop=True),
                    r=(lkey, skey), w=(f"ps{pk}",))
                P.s(lambda e, dstb=dstb, gg=gg, pk=pk: e.copy(
                    dstb[:, gg[0]:gg[1]], cx.ps[pk][:, :]),
                    r=(f"ps{pk}",), w=(dkey,))
        P.v(lambda e: e.tensor_mul(tm1[:, :], qfre[:, :], cfre[:, :]),
            r=("E2:qfre", "E2:cfre"), w=("E2:tm1",))
        P.g(lambda e: e.tensor_mul(tm2[:, :], qfim[:, :], cfim[:, :]),
            r=("E2:qfim", "E2:cfim"), w=("E2:tm2",))
        P.g(lambda e: e.tensor_mul(tm3[:, :], qfre[:, :], cfim[:, :]),
            r=("E2:qfre", "E2:cfim"), w=("E2:tm3",))
        P.v(lambda e: e.tensor_mul(tm4[:, :], qfim[:, :], cfre[:, :]),
            r=("E2:qfim", "E2:cfre"), w=("E2:tm4",))
        P.v(lambda e: e.tensor_sub(qvre[:, :], tm1[:, :], tm2[:, :]),
            r=("E2:tm1", "E2:tm2"), w=("E2:qvre",))
        P.g(lambda e: e.tensor_add(qvim[:, :], tm3[:, :], tm4[:, :]),
            r=("E2:tm3", "E2:tm4"), w=("E2:qvim",))
        for g in range(4):
            gg = (g*512, (g+1)*512)
            pk = g % 2
            P.t(lambda e, gg=gg, pk=pk: e.matmul(
                cx.ps[pk][:, :], cx.CM2[:, :], qvre[:, gg[0]:gg[1]],
                start=True, stop=False),
                r=("c:CM2", "E2:qvre"), w=(f"ps{pk}",))
            P.t(lambda e, gg=gg, pk=pk: e.matmul(
                cx.ps[pk][:, :], cx.SM2[:, :], qvim[:, gg[0]:gg[1]],
                start=False, stop=True),
                r=("c:SM2", "E2:qvim"), w=(f"ps{pk}",))
            P.s(lambda e, gg=gg, pk=pk: e.copy(
                vhb[:, gg[0]:gg[1]], cx.ps[pk][:, :]),
                r=(f"ps{pk}",), w=("E2:vhb",))
        P.dma("sync", lambda e, r0=r0, c0=c0: e.dma_start(
            io["vh_s"][r0:r0+128, c0:c0+T], vhb[:, :]),
            r=("E2:vhb",), w=())

    emit_loads(0)
    stage_a(0)
    emit_loads(1)
    for p in range(len(pairs)):
        if p + 2 < len(pairs):
            emit_loads(p + 2)
        if p + 1 < len(pairs):
            stage_a(p + 1)
        stage_b(p)


def emit_phaseF(P, cx, stack):
    nc, io = cx.nc, cx.io
    sb = lambda n, s, d: stack.enter_context(nc.sbuf_tensor(n, s, d))
    ysl = sb("F_ysl", [128, NKI * 512], BF16)
    f1stack = ExitStack()
    sb1 = lambda n, s, d: f1stack.enter_context(nc.sbuf_tensor(n, s, d))
    wtbuf = [sb1(f"F_wt{i}", [128, NEXP], F32) for i in range(4)]
    cell = [sb1(f"F_cell{i}", [128, 4], F32) for i in range(4)]
    ebuf = [sb1(f"F_eb{i}", [128, NEXP], F32) for i in range(4)]
    probs = [sb1(f"F_probs{i}", [128, NEXP], F32) for i in range(4)]
    probs15 = [sb1(f"F_p15{i}", [128, NEXP], F32) for i in range(4)]
    biased = [sb1(f"F_biased{i}", [128, NEXP], F32) for i in range(4)]
    mask1 = [sb1(f"F_msk1{i}", [128, NEXP], F32) for i in range(4)]
    mask2 = ebuf
    wtsT = sb1("F_wtsT", [12, NT], F32)
    y_r = io["y3_s"].rearrange("(k p) t -> p k t", p=128)
    out_r = io["out"].rearrange("(k p) t -> p k t", p=128)

    I4 = range(4)
    for tg in range(8):
        c0 = tg * 512
        for q in range(4):
            P.dma("sync", lambda e, q=q, c0=c0: e.dma_start(
                ysl[:, :].rearrange("p (k t) -> p k t", t=512)[:, 4*q:4*q+4, :],
                y_r[:, 4*q:4*q+4, c0:c0+512]),
                w=(f"F:ysl{q}",))
        ykeys = tuple(f"F:ysl{q}" for q in range(4))
        for tc in I4:
            for ki in range(16):
                P.t(lambda e, tc=tc, ki=ki: e.matmul(
                    cx.ps[tc][:, 0:NEXP],
                    ysl[:, ki*512+tc*128:ki*512+(tc+1)*128],
                    cx.cls[:, ki*NEXP:(ki+1)*NEXP],
                    start=(ki == 0), stop=(ki == 15)),
                    r=ykeys + ("c:cls",), w=(f"ps{tc}",))
        for tc in I4:
            P.v(lambda e, tc=tc: e.reduce_max(cell[tc][:, 0:1],
                                              cx.ps[tc][:, 0:NEXP], AX.X),
                r=(f"ps{tc}",), w=(f"F:cell{tc}",))
        for tc in I4:
            P.v(lambda e, tc=tc: e.tensor_scalar_mul(cell[tc][:, 1:2],
                                                     cell[tc][:, 0:1], -1.0),
                r=(f"F:cell{tc}",), w=(f"F:cell{tc}",))
        for tc in I4:
            P.s(lambda e, tc=tc: e.activation(ebuf[tc][:, :],
                                              cx.ps[tc][:, 0:NEXP], AF.Exp,
                                              bias=cell[tc][:, 1:2]),
                r=(f"ps{tc}", f"F:cell{tc}"), w=(f"F:eb{tc}",))
        for tc in I4:
            P.v(lambda e, tc=tc: e.reduce_sum(cell[tc][:, 2:3],
                                              ebuf[tc][:, :], AX.X),
                r=(f"F:eb{tc}",), w=(f"F:cell{tc}",))
        for tc in I4:
            P.v(lambda e, tc=tc: e.reciprocal(cell[tc][:, 3:4],
                                              cell[tc][:, 2:3]),
                r=(f"F:cell{tc}",), w=(f"F:cell{tc}",))
        for tc in I4:
            P.v(lambda e, tc=tc: e.tensor_scalar_mul(probs[tc][:, :],
                                                     ebuf[tc][:, :],
                                                     cell[tc][:, 3:4]),
                r=(f"F:eb{tc}", f"F:cell{tc}"), w=(f"F:probs{tc}",))
        for tc in I4:
            P.v(lambda e, tc=tc: e.tensor_scalar(probs15[tc][:, :],
                                                 ebuf[tc][:, :],
                                                 cell[tc][:, 3:4], 1.5,
                                                 ALU.mult, ALU.mult),
                r=(f"F:eb{tc}", f"F:cell{tc}"), w=(f"F:p15{tc}",))
        for tc in I4:
            P.v(lambda e, tc=tc: e.tensor_add(biased[tc][:, :],
                                              probs[tc][:, :], cx.ebias[:, :]),
                r=(f"F:probs{tc}", "c:ebias"), w=(f"F:biased{tc}",))
        for tc in I4:
            P.v(lambda e, tc=tc: e.reduce_max(cell[tc][:, 0:1],
                                              biased[tc][:, :], AX.X),
                r=(f"F:biased{tc}",), w=(f"F:cell{tc}",))
        for tc in I4:
            P.v(lambda e, tc=tc: e.tensor_scalar_sub(mask1[tc][:, :],
                                                     biased[tc][:, :],
                                                     cell[tc][:, 0:1]),
                r=(f"F:biased{tc}", f"F:cell{tc}"), w=(f"F:m1{tc}",))
        for tc in I4:
            P.v(lambda e, tc=tc: e.tensor_single_scalar(mask1[tc][:, :],
                                                        mask1[tc][:, :],
                                                        0.0, ALU.is_ge),
                r=(f"F:m1{tc}",), w=(f"F:m1{tc}",))
        for tc in I4:
            P.v(lambda e, tc=tc: e.scalar_tensor_tensor(
                biased[tc][:, :], mask1[tc][:, :], -1e30, biased[tc][:, :],
                ALU.mult, ALU.add),
                r=(f"F:m1{tc}", f"F:biased{tc}"), w=(f"F:biased{tc}",))
        for tc in I4:
            P.v(lambda e, tc=tc: e.reduce_max(cell[tc][:, 0:1],
                                              biased[tc][:, :], AX.X),
                r=(f"F:biased{tc}",), w=(f"F:cell{tc}",))
        for tc in I4:
            P.v(lambda e, tc=tc: e.tensor_scalar_sub(mask2[tc][:, :],
                                                     biased[tc][:, :],
                                                     cell[tc][:, 0:1]),
                r=(f"F:biased{tc}", f"F:cell{tc}"),
                w=(f"F:m2{tc}", f"F:eb{tc}"))
        for tc in I4:
            P.v(lambda e, tc=tc: e.tensor_single_scalar(mask2[tc][:, :],
                                                        mask2[tc][:, :],
                                                        0.0, ALU.is_ge),
                r=(f"F:m2{tc}",), w=(f"F:m2{tc}",))
        for tc in I4:
            P.v(lambda e, tc=tc: e.tensor_add(mask1[tc][:, :], mask1[tc][:, :],
                                              mask2[tc][:, :]),
                r=(f"F:m1{tc}", f"F:m2{tc}"), w=(f"F:m1{tc}",))
        for tc in I4:
            P.v(lambda e, tc=tc: e.tensor_mul(wtbuf[tc][:, 0:NEXP],
                                              mask1[tc][:, :],
                                              probs15[tc][:, :]),
                r=(f"F:m1{tc}", f"F:p15{tc}"), w=(f"F:wt{tc}",))
        for tc in I4:
            pk = 4 + tc % 2
            P.t(lambda e, tc=tc, pk=pk: e.transpose(
                cx.ps[pk][0:NEXP, 0:128], wtbuf[tc][:, 0:NEXP],
                cx.ident[:, :]),
                r=(f"F:wt{tc}", "c:ident"), w=(f"ps{pk}",))
            P.v(lambda e, c0=c0, tc=tc, pk=pk: e.tensor_copy(
                wtsT[0:NEXP, c0+tc*128:c0+(tc+1)*128],
                cx.ps[pk][0:NEXP, 0:128]),
                r=(f"ps{pk}",), w=("F:wtsT",))
    P.dma("sync", lambda e: e.dma_start(io["wts_s"][:, :], wtsT[0:NEXP, :]),
          r=("F:wtsT",), w=("F:wts_s",))
    f1stack.close()
    P.barrier()
    webuf = sb("F_webuf", [1, NEXP * 512], BF16)
    z1b = sb("F_z1b", [1, 512], BF16)
    gwb = [sb(f"F_gw{i}", [128, NKI * 128], BF16) for i in range(3)]
    uwb = [sb(f"F_uw{i}", [128, NKI * 128], BF16) for i in range(3)]
    dwb = [sb(f"F_dw{i}", [128, 8 * 128], BF16) for i in range(3)]
    sigb = sb("F_sig", [128, 512], BF16)
    gsb = sb("F_gs", [128, 512], BF16)
    uw2 = sb("F_uw2b", [128, 512], BF16)
    sbwe = [sb(f"F_sbwe{i}", [128, 512], F32) for i in range(2)]
    actb = [sb(f"F_act{i}", [128, 8 * 512], BF16) for i in range(2)]
    ln = LNKit(P, cx, stack, "F", out_f32=True)
    xr = ln.xr

    for tg in range(8):
        c0 = tg * 512
        for q in range(4):
            P.dma("sync", lambda e, q=q, c0=c0: e.dma_start(
                ysl[:, :].rearrange("p (k t) -> p k t", t=512)[:, 4*q:4*q+4, :],
                y_r[:, 4*q:4*q+4, c0:c0+512]),
                w=(f"F:ysl{q}",))
        ykeys = tuple(f"F:ysl{q}" for q in range(4))
        P.dma("gpsimd", lambda e, c0=c0: e.dma_start(
            webuf[0:1, :].rearrange("p (ee t) -> p ee t", t=512),
            io["wts_s"][:, c0:c0+512].rearrange("ee t -> () ee t")),
            r=("F:wts_s",), w=("F:webuf",))
        P.v(lambda e: e.tensor_add(z1b[0:1, :], webuf[0:1, 8*512:9*512],
                                   webuf[0:1, 9*512:10*512]),
            r=("F:webuf",), w=("F:z1b",))
        P.v(lambda e: e.tensor_add(z1b[0:1, :], z1b[0:1, :],
                                   webuf[0:1, 10*512:11*512]),
            r=("F:webuf", "F:z1b"), w=("F:z1b",))
        P.v(lambda e: e.tensor_add(z1b[0:1, :], z1b[0:1, :],
                                   webuf[0:1, 11*512:12*512]),
            r=("F:webuf", "F:z1b"), w=("F:z1b",))
        P.v(lambda e: e.tensor_scalar_add(z1b[0:1, :], z1b[0:1, :], 1.0),
            r=("F:z1b",), w=("F:z1b",))
        P.t(lambda e: e.matmul(cx.ps[4][:, :], cx.ones_row[:, :],
                               z1b[0:1, :], start=True, stop=True),
            r=("F:z1b", "c:ones_row"), w=("ps4",))
        for mo in range(16):
            P.v(lambda e, mo=mo: e.tensor_mul(
                xr[:, mo*512:(mo+1)*512], ysl[:, mo*512:(mo+1)*512],
                cx.ps[4][:, :]),
                r=ykeys + ("ps4",), w=(f"F:xr{mo}",))
        def stage_gu(ex):
            al = ex % 2
            P.t(lambda e, ex=ex: e.matmul(
                cx.ps[5][:, :], cx.ones_row[:, :],
                webuf[0:1, ex*512:(ex+1)*512], start=True, stop=True),
                r=("F:webuf", "c:ones_row"), w=("ps5",))
            P.v(lambda e, al=al: e.tensor_copy(sbwe[al][:, :], cx.ps[5][:, :]),
                r=("ps5",), w=(f"F:sbwe{al}",))
            for i_o in range(8):
                sl = (ex * 8 + i_o) % 3
                P.dma("scalar", lambda e, ex=ex, i_o=i_o, sl=sl: e.dma_start(
                    gwb[sl][:, :].rearrange("p (k j) -> p k j", j=128),
                    io["gw_b"][ex, i_o]),
                    r=("wc:gw_b",), w=(f"F:gw{sl}",))
                P.dma("scalar", lambda e, ex=ex, i_o=i_o, sl=sl: e.dma_start(
                    uwb[sl][:, :].rearrange("p (k j) -> p k j", j=128),
                    io["uw_b"][ex, i_o]),
                    r=("wc:uw_b",), w=(f"F:uw{sl}",))
                pg = i_o % 2
                pu = 2 + i_o % 2
                for ki in range(16):
                    P.t(lambda e, ki=ki, sl=sl, pg=pg: e.matmul(
                        cx.ps[pg][:, :], gwb[sl][:, ki*128:(ki+1)*128],
                        ysl[:, ki*512:(ki+1)*512],
                        start=(ki == 0), stop=(ki == 15)),
                        r=(f"F:gw{sl}",) + ykeys, w=(f"ps{pg}",))
                for ki in range(16):
                    P.t(lambda e, ki=ki, sl=sl, pu=pu: e.matmul(
                        cx.ps[pu][:, :], uwb[sl][:, ki*128:(ki+1)*128],
                        ysl[:, ki*512:(ki+1)*512],
                        start=(ki == 0), stop=(ki == 15)),
                        r=(f"F:uw{sl}",) + ykeys, w=(f"ps{pu}",))
                P.s(lambda e, pg=pg: e.activation(
                    sigb[:, :], cx.ps[pg][:, :], AF.Sigmoid),
                    r=(f"ps{pg}",), w=("F:sig",))
                P.v(lambda e, pg=pg: e.tensor_mul(
                    gsb[:, :], cx.ps[pg][:, :], sigb[:, :]),
                    r=(f"ps{pg}", "F:sig"), w=("F:gs",))
                P.v(lambda e, pu=pu, al=al: e.tensor_mul(
                    uw2[:, :], cx.ps[pu][:, :], sbwe[al][:, :]),
                    r=(f"ps{pu}", f"F:sbwe{al}"), w=("F:uw2",))
                P.v(lambda e, i_o=i_o, al=al: e.tensor_mul(
                    actb[al][:, i_o*512:(i_o+1)*512], gsb[:, :], uw2[:, :]),
                    r=("F:gs", "F:uw2"), w=(f"F:act{al}_{i_o}",))

        def stage_down(ex):
            al = ex % 2
            actkeys = tuple(f"F:act{al}_{i}" for i in range(8))
            for mo in range(16):
                sl = (ex * 16 + mo) % 3
                pk = 6 + mo % 2
                P.dma("gpsimd", lambda e, ex=ex, mo=mo, sl=sl: e.dma_start(
                    dwb[sl][:, :].rearrange("p (k j) -> p k j", j=128),
                    io["dw_b"][ex, mo]),
                    r=("wc:dw_b",), w=(f"F:dw{sl}",))
                for ki in range(8):
                    P.t(lambda e, ki=ki, sl=sl, pk=pk, al=al: e.matmul(
                        cx.ps[pk][:, :], dwb[sl][:, ki*128:(ki+1)*128],
                        actb[al][:, ki*512:(ki+1)*512],
                        start=(ki == 0), stop=(ki == 7)),
                        r=(f"F:dw{sl}",) + actkeys, w=(f"ps{pk}",))
                P.v(lambda e, mo=mo, pk=pk: e.tensor_add(
                    xr[:, mo*512:(mo+1)*512], xr[:, mo*512:(mo+1)*512],
                    cx.ps[pk][:, :]),
                    r=(f"ps{pk}", f"F:xr{mo}"), w=(f"F:xr{mo}",))

        stage_gu(0)
        for ex in range(NE):
            if ex + 1 < NE:
                stage_gu(ex + 1)
            stage_down(ex)
        ln.emit(c0, out_r)



PHASE_IO = {
    "A": (("xT",), ("q_s", "k_s", "v_s")),
    "B": (("q_s", "k_s", "v_s"), ("attn_s",)),
    "C": (("attn_s", "xT"), ("y1_s",)),
    "D1": (("y1_s",), ("act_s",)),
    "D3": (("act_s", "y1_s"), ("y2_s",)),
    "E1": (("y2_s",), ("q2_s", "kv2_s")),
    "E2": (("q2_s", "kv2_s"), ("vh_s",)),
    "E3": (("vh_s", "y2_s"), ("y3_s",)),
    "F": (("y3_s",), ("wts_s", "out")),
}

PHASE_EMIT = {}


def register_phases():
    PHASE_EMIT["A"] = emit_phaseA
    PHASE_EMIT["B"] = emit_phaseB
    PHASE_EMIT["C"] = lambda P, cx, st: emit_out_proj_ln(
        P, cx, st, w_t="wo_t", src="attn_s", resid="xT", dst="y1_s",
        use_bias=True)
    PHASE_EMIT["E3"] = lambda P, cx, st: emit_out_proj_ln(
        P, cx, st, w_t="l1wo_t", src="vh_s", resid="y2_s", dst="y3_s",
        use_bias=False)
    for nm, ph in (("emit_phaseD1", "D1"), ("emit_phaseD3", "D3"),
                   ("emit_phaseE1", "E1"), ("emit_phaseE2", "E2"),
                   ("emit_phaseF", "F")):
        if nm in globals():
            PHASE_EMIT[ph] = globals()[nm]


def run_phases_launch(phase_names, host_arrays, percore, trace=False):
    stack = ExitStack()
    try:
        nc, cx = setup_program(stack, phase_names, set(SCRATCH_SPECS))
        P = Prog()
        emit_init(P, cx)
        P.barrier()
        for ph in phase_names:
            PHASE_EMIT[ph](P, cx, stack)
            P.barrier()
        P.emit(nc, stack)
    finally:
        pass
    reads, writes = set(), set()
    for ph in phase_names:
        r, w = PHASE_IO[ph]
        reads |= set(r)
        writes |= set(w)
    in_maps = []
    for c in range(8):
        m = {}
        m["xT"] = percore[c]["xT"]
        for nm in SHARED_SPECS:
            m[nm] = host_arrays[nm]
        for nm in SCRATCH_SPECS:
            if nm in reads and nm not in writes:
                m[nm] = percore[c][nm]
        in_maps.append(m)
    res = run_bass_kernel_spmd(nc, in_maps, core_ids=list(range(8)),
                               trace=trace)
    stack.close()
    for c in range(8):
        for nm, arr in res.results[c].items():
            percore[c][nm] = arr
    return res


PHASE_ORDER = ("A", "B", "C", "D1", "D3", "E1", "E2", "E3", "F")
FUSED = True


def run_fused(host_arrays, percore, trace=False):
    stack = ExitStack()
    try:
        nc, cx = setup_program(stack, list(PHASE_ORDER), set())
        P = Prog()
        emit_init(P, cx)
        P.barrier()
        for ph in PHASE_ORDER:
            with ExitStack() as phstack:
                PHASE_EMIT[ph](P, cx, phstack)
            P.barrier()
        P.emit(nc, stack)
        in_maps = []
        for c in range(8):
            m = {"xT": percore[c]["xT"]}
            for nm in SHARED_SPECS:
                m[nm] = host_arrays[nm]
            in_maps.append(m)
        res = run_bass_kernel_spmd(nc, in_maps, core_ids=list(range(8)),
                                   trace=trace)
    finally:
        stack.close()
    for c in range(8):
        for nm, arr in res.results[c].items():
            percore[c][nm] = arr
    return res


def kernel(**inputs):
    register_phases()
    sh, percore = prep_host(inputs)
    if FUSED:
        run_fused(sh, percore)
    else:
        for ph in PHASE_ORDER:
            run_phases_launch([ph], sh, percore)
    out = np.zeros((16, 2048, 2048), np.float32)
    for c in range(8):
        r = percore[c]["out"]
        for bl in range(NB):
            out[2 * c + bl] = r[:, bl * T:(bl + 1) * T].T
    return out


# version 06 (final)
# speedup vs baseline: 1.0439x; 1.0439x over previous
import math
import numpy as np
import ml_dtypes
import concourse.bass as bass
import concourse.mybir as mybir
from concourse.bass_utils import run_bass_kernel_spmd
from contextlib import ExitStack

F32 = mybir.dt.float32
BF16 = mybir.dt.bfloat16
I8 = mybir.dt.int8
AF = mybir.ActivationFunctionType
ALU = mybir.AluOpType
AX = mybir.AxisListType
BF = ml_dtypes.bfloat16

D = 2048
NKI = 16
T = 2048
NB = 2
NT = NB * T
NTG = 8
H = 16
HD = 128
INTER = 8192
I2 = 1024
NE = 8
NEXP = 12
SCALE_ATT = 1.0 / math.sqrt(128.0)
ENGINES = ("tensor", "vector", "scalar", "gpsimd", "sync")


class Prog:
    def __init__(self):
        self.ops = []

    def _op(self, engine, fn, r, w, dma, sync=False, chain=None):
        r = tuple(r)
        w = tuple(w)
        psr = tuple(k for k in r if k.startswith("ps"))
        if psr:
            r = tuple(k for k in r if not k.startswith("ps"))
            w = w + psr
        self.ops.append([engine, fn, r, w, dma, False, sync, chain])

    def t(self, fn, r=(), w=(), sync=False):
        self._op("tensor", fn, r, w, False, sync)

    def v(self, fn, r=(), w=(), sync=False):
        self._op("vector", fn, r, w, False, sync)

    def s(self, fn, r=(), w=(), sync=False):
        self._op("scalar", fn, r, w, False, sync)

    def g(self, fn, r=(), w=(), sync=False):
        self._op("gpsimd", fn, r, w, False, sync)

    def dma(self, queue, fn, r=(), w=(), chain=None):
        self._op(queue, fn, r, w, True, chain=chain)

    def barrier(self):
        self.ops.append(["BAR", None, (), (), False, False, False, None])

    def schedule(self):
        n = len(self.ops)
        deps = [set() for _ in range(n)]
        last_w = {}
        readers = {}
        last_on = {}
        bar_deps = []
        bar_edges = set()
        seen_since_bar = set()
        for i, op in enumerate(self.ops):
            eng, _fn, rds, wrs, dma, _, syncf = op[:7]
            if eng == "BAR":
                bar_deps = [j for j in last_on.values() if j is not None]
                seen_since_bar = set()
                continue
            if eng not in seen_since_bar:
                for j in bar_deps:
                    if self.ops[j][0] != eng or self.ops[j][4] or dma:
                        deps[i].add(j)
                        bar_edges.add((i, j))
                seen_since_bar.add(eng)
            for r in rds:
                if r in last_w:
                    deps[i].add(last_w[r])
                readers.setdefault(r, []).append(i)
            for w in wrs:
                if w in last_w:
                    deps[i].add(last_w[w])
                for j in readers.get(w, ()):
                    deps[i].add(j)
            if not syncf:
                deps[i] = {j for j in deps[i]
                           if not (self.ops[j][0] == eng and not self.ops[j][4]
                                   and not dma)}
            for w in wrs:
                last_w[w] = i
                readers[w] = []
            last_on[eng] = i
        chains = {}
        for idx, op in enumerate(self.ops):
            if op[0] != "BAR" and op[7] is not None:
                chains.setdefault(op[7], []).append(idx)
        nxt = {}
        import os as _os
        if _os.environ.get("NOCHAIN", "") != "1":
            for lst in chains.values():
                for a, b in zip(lst, lst[1:]):
                    nxt[a] = b
        for i, op in enumerate(self.ops):
            if op[0] == "BAR":
                continue
            nd = set()
            for j in deps[i]:
                cj = self.ops[j][7]
                if (cj is not None and op[7] != cj and j in nxt
                        and (i, j) not in bar_edges):
                    nd.add(nxt[j])
                else:
                    nd.add(j)
            best = {}
            for j in nd:
                se = self.ops[j][0]
                if se not in best or j > best[se]:
                    best[se] = j
            deps[i] = set(best.values())
            for j in deps[i]:
                self.ops[j][5] = True
        cnt = {e: 0 for e in ENGINES}
        count_after = {}
        for i, op in enumerate(self.ops):
            eng = op[0]
            if eng == "BAR":
                continue
            if op[4]:
                cnt[eng] += 16
                count_after[i] = cnt[eng]
            elif op[5]:
                cnt[eng] += 1
                count_after[i] = cnt[eng]
        self.deps = deps
        self.count_after = count_after
        self.totals = cnt

    def emit(self, nc, stack):
        self.schedule()
        sems = {e: stack.enter_context(nc.semaphore(f"sem_{e}")) for e in ENGINES}
        blk = stack.enter_context(nc.Block())
        per_engine = {e: [] for e in ENGINES}
        for i, op in enumerate(self.ops):
            if op[0] != "BAR":
                per_engine[op[0]].append(i)

        def run_engine(eng_name):
            def body(eng):
                waited = {e: 0 for e in ENGINES}
                for i in per_engine[eng_name]:
                    op = self.ops[i]
                    ds = sorted(self.deps[i], key=lambda j: self.count_after[j])
                    fresh_compute = False
                    for j in ds:
                        src = self.ops[j][0]
                        need = self.count_after[j]
                        if waited[src] < need:
                            eng.wait_ge(sems[src], need)
                            waited[src] = need
                            if src in ("vector", "scalar", "gpsimd") \
                                    and not self.ops[j][4]:
                                fresh_compute = True
                    if op[4] and fresh_compute:
                        eng.nop(cycle_cnt=700, nofuse=True)
                    inst = op[1](eng)
                    if op[4]:
                        inst.then_inc(sems[eng_name], 16)
                    elif op[5]:
                        inst.then_inc(sems[eng_name], 1)
                if eng_name == "sync":
                    for q in ENGINES:
                        if self.totals[q] > 0 and q != "sync":
                            eng.wait_ge(sems[q], self.totals[q])
                    if self.totals["sync"] > 0:
                        eng.wait_ge(sems["sync"], self.totals["sync"])
            return body

        blk.tensor(run_engine("tensor"))
        blk.vector(run_engine("vector"))
        blk.scalar(run_engine("scalar"))
        blk.gpsimd(run_engine("gpsimd"))
        blk.sync(run_engine("sync"))


def chain_tail(P, cx, queue, src_ap, chain, f32=False):
    dst = cx.dummyf if f32 else cx.dummy
    P.dma(queue, lambda e, a=src_ap, d=dst: e.dma_start(d[:, 0:32], a),
          r=(), w=(), chain=chain)


def tiles_lhsT(w):
    Din, Dout = w.shape
    Ki, Mo = Din // 128, Dout // 128
    return np.ascontiguousarray(
        w.reshape(Ki, 128, Mo, 128).transpose(2, 1, 0, 3))


def perpart(vec):
    Mo = vec.shape[0] // 128
    return np.ascontiguousarray(vec.reshape(Mo, 128).T)


def make_consts():
    t = np.arange(128)
    ang = 2.0 * np.pi * np.outer(t, t) / 128.0
    CM = np.cos(ang).astype(np.float32)
    SM = (-np.sin(ang)).astype(np.float32)
    mask4 = np.zeros((4, 128, 512), np.float32)
    for p in range(4):
        tk = np.arange(128)[:, None]
        tq = np.arange(512)[None, :]
        mask4[p] = np.where(tq >= 128 * p + tk, 0.0, -1e9).astype(np.float32)
    ident = np.eye(128, dtype=np.float32)
    return CM, SM, mask4, ident


def prep_host(inputs):
    CM, SM, mask4, ident = make_consts()
    sh = {}
    sh["wq_t"] = tiles_lhsT(inputs["L0_attn_w_q"])
    sh["wk_t"] = tiles_lhsT(inputs["L0_attn_w_k"])
    sh["wv_r"] = np.ascontiguousarray(
        inputs["L0_attn_w_v"].reshape(NKI, 128, D))
    sh["wo_t"] = tiles_lhsT(inputs["L0_attn_w_o"])
    sh["bq_p"] = perpart(inputs["L0_attn_b_q"])
    sh["bk_p"] = perpart(inputs["L0_attn_b_k"])
    sh["bo_p"] = perpart(inputs["L0_attn_b_o"])
    sh["bv_p"] = perpart(inputs["L0_attn_b_v"])
    sh["wg_t"] = tiles_lhsT(inputs["L0_ffn_w_gate"])
    sh["wu_t"] = tiles_lhsT(inputs["L0_ffn_w_up"])
    sh["wd_t"] = tiles_lhsT(inputs["L0_ffn_w_down"])
    sh["sg_p"] = perpart(inputs["L0_ffn_s_gate"])
    sh["su_p"] = perpart(inputs["L0_ffn_s_up"])
    sh["sd_p"] = perpart(inputs["L0_ffn_s_down"])
    sh["l1wq_t"] = tiles_lhsT(np.ascontiguousarray(inputs["L1_attn_W_q"].T))
    sh["l1wkv_t"] = tiles_lhsT(np.ascontiguousarray(inputs["L1_attn_W_kv"].T))
    sh["l1wo_t"] = tiles_lhsT(np.ascontiguousarray(inputs["L1_attn_W_o"].T))
    cls = inputs["L1_ffn_classifier_w"]
    sh["cls_r"] = np.ascontiguousarray(
        cls.reshape(NEXP, NKI, 128).transpose(2, 1, 0)).reshape(128, NKI * NEXP)
    sh["ebias_t"] = np.ascontiguousarray(
        np.broadcast_to(inputs["L1_ffn_e_score_correction_bias"][None, :],
                        (128, NEXP)))
    gw = inputs["L1_ffn_gate_w"]
    uw = inputs["L1_ffn_up_w"]
    dw = inputs["L1_ffn_down_w"]
    sh["gw_t"] = np.stack([tiles_lhsT(np.ascontiguousarray(gw[e].T))
                           for e in range(NE)])
    sh["uw_t"] = np.stack([tiles_lhsT(np.ascontiguousarray(uw[e].T))
                           for e in range(NE)])
    sh["dw_t"] = np.stack([tiles_lhsT(np.ascontiguousarray(dw[e].T))
                           for e in range(NE)])
    sh["CM"] = CM
    sh["SM"] = SM
    sh["CM2"] = CM / 128.0
    sh["SM2"] = SM / 128.0
    sh["mask4"] = np.ascontiguousarray(mask4.transpose(1, 0, 2).reshape(128, 4 * 512))
    sh["ident"] = ident

    x = inputs["L0_attn_query"]
    percore = []
    for c in range(8):
        xc = x[2 * c:2 * c + 2].reshape(NT, D)
        percore.append({"xT": np.ascontiguousarray(xc.T)})
    return sh, percore


SHARED_SPECS = {
    "wq_t": ([16, 128, 16, 128], F32), "wk_t": ([16, 128, 16, 128], F32),
    "wv_r": ([16, 128, D], F32), "wo_t": ([16, 128, 16, 128], F32),
    "bq_p": ([128, 16], F32), "bk_p": ([128, 16], F32),
    "bo_p": ([128, 16], F32), "bv_p": ([128, 16], F32),
    "wg_t": ([64, 128, 16, 128], I8), "wu_t": ([64, 128, 16, 128], I8),
    "wd_t": ([16, 128, 64, 128], I8),
    "sg_p": ([128, 64], F32), "su_p": ([128, 64], F32), "sd_p": ([128, 16], F32),
    "l1wq_t": ([16, 128, 16, 128], F32), "l1wkv_t": ([16, 128, 16, 128], F32),
    "l1wo_t": ([16, 128, 16, 128], F32),
    "cls_r": ([128, NKI * NEXP], F32), "ebias_t": ([128, NEXP], F32),
    "gw_t": ([NE, 8, 128, 16, 128], F32), "uw_t": ([NE, 8, 128, 16, 128], F32),
    "dw_t": ([NE, 16, 128, 8, 128], F32),
    "CM": ([128, 128], F32), "SM": ([128, 128], F32),
    "CM2": ([128, 128], F32), "SM2": ([128, 128], F32),
    "mask4": ([128, 4 * 512], F32), "ident": ([128, 128], F32),
}
SCRATCH_SPECS = {
    "q_s": ([D, NT], BF16), "k_s": ([D, NT], BF16),
    "v_s": ([NT // 128, 128, D], BF16),
    "attn_s": ([D, NT], BF16),
    "y1_s": ([D, NT], BF16),
    "act_s": ([INTER, NT], BF16),
    "y2_s": ([D, NT], BF16),
    "q2_s": ([D, NT], BF16), "kv2_s": ([D, NT], BF16),
    "vh_s": ([D, NT], BF16),
    "y3_s": ([D, NT], BF16),
    "wts_s": ([NEXP, NT], F32),
    "wq_b": ([16, 128, 16, 128], BF16), "wk_b": ([16, 128, 16, 128], BF16),
    "wv_b": ([16, 128, D], BF16), "wo_b": ([16, 128, 16, 128], BF16),
    "l1wq_b": ([16, 128, 16, 128], BF16), "l1wkv_b": ([16, 128, 16, 128], BF16),
    "l1wo_b": ([16, 128, 16, 128], BF16),
    "gw_b": ([NE, 8, 128, 16, 128], BF16), "uw_b": ([NE, 8, 128, 16, 128], BF16),
    "dw_b": ([NE, 16, 128, 8, 128], BF16),
    "wd_b": ([16, 128, 64, 128], BF16),
    "x_b": ([D, NT], BF16),
}


class Ctx:
    pass


def setup_program(stack, phase_names, external_scratch):
    nc = bass.Bass("TRN2", target_bir_lowering=False)
    cx = Ctx()
    cx.nc = nc
    cx.io = {}
    cx.io["xT"] = nc.dram_tensor("xT", [D, NT], F32, kind="ExternalInput").ap()
    for nm, (shape, dt) in SHARED_SPECS.items():
        cx.io[nm] = nc.dram_tensor(nm, shape, dt, kind="ExternalInput").ap()
    reads, writes = set(), set()
    for ph in phase_names:
        r, w = PHASE_IO[ph]
        reads |= set(r)
        writes |= set(w)
    for nm, (shape, dt) in SCRATCH_SPECS.items():
        if nm.endswith("_b"):
            cx.io[nm] = nc.dram_tensor(nm, shape, dt, kind="Internal").ap()
        elif nm in external_scratch:
            if nm in writes:
                cx.io[nm] = nc.dram_tensor(nm, shape, dt,
                                           kind="ExternalOutput").ap()
            elif nm in reads:
                cx.io[nm] = nc.dram_tensor(nm, shape, dt,
                                           kind="ExternalInput").ap()
        elif (nm in reads) or (nm in writes):
            cx.io[nm] = nc.dram_tensor(nm, shape, dt, kind="Internal").ap()
    if "out" in writes:
        cx.io["out"] = nc.dram_tensor("out", [D, NT], F32,
                                      kind="ExternalOutput").ap()
    cx.psD = [stack.enter_context(nc.psum_tensor(f"psd{i}", [128, 1024], F32))
              for i in range(2)]
    cx.ps = [cx.psD[0][:, 0:512], cx.psD[0][:, 512:1024],
             cx.psD[1][:, 0:512], cx.psD[1][:, 512:1024]]
    cx.ps += [stack.enter_context(nc.psum_tensor(f"psb{i}", [128, 512], F32))[:, :]
              for i in range(4, 8)]
    sb = lambda name, shape, dt: stack.enter_context(
        nc.sbuf_tensor(name, shape, dt))
    cx.ones_col = sb("ones_col", [128, 1], BF16)
    cx.ones_row = sb("ones_row", [1, 128], BF16)
    cx.masks = sb("masks", [128, 4 * 512], BF16)
    cx.CM = sb("cCM", [128, 128], BF16)
    cx.SM = sb("cSM", [128, 128], BF16)
    cx.CM2 = sb("cCM2", [128, 128], BF16)
    cx.SM2 = sb("cSM2", [128, 128], BF16)
    cx.ident = sb("cident", [128, 128], F32)
    cx.dummy = sb("cdummy", [128, 32], BF16)
    cx.dummyf = sb("cdummyf", [128, 32], F32)
    cx.identb = sb("cidentb", [128, 128], BF16)
    cx.bq = sb("cbq", [128, 16], F32)
    cx.bk = sb("cbk", [128, 16], F32)
    cx.bo = sb("cbo", [128, 16], F32)
    cx.bvb = sb("cbvb", [128, 16], BF16)
    cx.sg = sb("csg", [128, 64], F32)
    cx.su = sb("csu", [128, 64], F32)
    cx.sd = sb("csd", [128, 16], F32)
    cx.cls = sb("ccls", [128, NKI * NEXP], BF16)
    cx.ebias = sb("cebias", [128, NEXP], F32)
    cx.sb = sb
    return nc, cx


def emit_init(P, cx):
    io = cx.io

    def cast_load(buf, src, key):
        P.dma("gpsimd", lambda e, b=buf, s=src: e.dma_start(b[:, :], s),
              r=(), w=(key,))

    def plain_load(buf, src, key):
        P.dma("sync", lambda e, b=buf, s=src: e.dma_start(b[:, :], s),
              r=(), w=(key,))

    P.v(lambda e: e.memset(cx.ones_col[:, :], 1.0), w=("c:ones_col",))
    P.v(lambda e: e.memset(cx.ones_row[:, :], 1.0), w=("c:ones_row",))
    cast_load(cx.masks, io["mask4"], "c:masks")
    cast_load(cx.CM, io["CM"], "c:CM")
    cast_load(cx.SM, io["SM"], "c:SM")
    cast_load(cx.CM2, io["CM2"], "c:CM2")
    cast_load(cx.SM2, io["SM2"], "c:SM2")
    cast_load(cx.bvb, io["bv_p"], "c:bvb")
    cast_load(cx.cls, io["cls_r"], "c:cls")
    cast_load(cx.identb, io["ident"], "c:identb")
    plain_load(cx.ident, io["ident"], "c:ident")
    convs = [("xT", "x_b", 16, None), ("wq_t", "wq_b", 16, None),
             ("wk_t", "wk_b", 16, None), ("wv_r", "wv_b", 8, None),
             ("wo_t", "wo_b", 16, None), ("wd_t", "wd_b", 16, None),
             ("l1wq_t", "l1wq_b", 16, None), ("l1wkv_t", "l1wkv_b", 16, None),
             ("l1wo_t", "l1wo_b", 16, None),
             ("gw_t", "gw_b", 16, "e a p k j -> (e a) p k j"),
             ("uw_t", "uw_b", 16, "e a p k j -> (e a) p k j"),
             ("dw_t", "dw_b", 8, None)]
    prev_key = None
    for src, dst, nch, rr in convs:
        key = f"wc:{'xb_s' if dst == 'x_b' else dst}"
        sap = io[src].rearrange(rr) if rr else io[src]
        dap = io[dst].rearrange(rr) if rr else io[dst]
        step = sap.shape[0] // nch
        for ci in range(nch):
            lo, hi = ci * step, (ci + 1) * step
            wkeys = (key, prev_key) if (ci == 0 and prev_key) else (key,)
            P.dma("gpsimd", lambda e, a=sap[lo:hi], b=dap[lo:hi]:
                  e.dma_start(b, a), r=(), w=wkeys)
        prev_key = key
    plain_load(cx.bq, io["bq_p"], "c:bq")
    plain_load(cx.bk, io["bk_p"], "c:bk")
    plain_load(cx.bo, io["bo_p"], "c:bo")
    plain_load(cx.sg, io["sg_p"], "c:sg")
    plain_load(cx.su, io["su_p"], "c:su")
    plain_load(cx.sd, io["sd_p"], "c:sd")
    plain_load(cx.ebias, io["ebias_t"], "c:ebias")


def emit_phaseA(P, cx, stack):
    nc, io = cx.nc, cx.io
    xslb = [stack.enter_context(
        nc.sbuf_tensor(f"A_xsl{i}", [128, NKI * 512], BF16)) for i in range(3)]
    ws = [stack.enter_context(nc.sbuf_tensor(f"A_ws{i}", [128, NKI * 128], BF16))
          for i in range(3)]
    wv = [stack.enter_context(nc.sbuf_tensor(f"A_wv{i}", [128, 1024], BF16))
          for i in range(4)]
    ob = [stack.enter_context(nc.sbuf_tensor(f"A_ob{i}", [128, 512], BF16))
          for i in range(4)]
    xT_r = io["xT"].rearrange("(k p) t -> p k t", p=128)

    for tg in range(8):
        c0 = tg * 512
        xs = tg % 3
        xsl = xslb[xs]
        for q in range(4):
            P.dma("gpsimd",
                  lambda e, q=q, c0=c0, xsl=xsl: e.dma_start(
                      xsl[:, :].rearrange("p (k t) -> p k t", t=512)[:, 4*q:4*q+4, :],
                      xT_r[:, 4*q:4*q+4, c0:c0 + 512]),
                  r=(), w=(f"A:xsl{xs}{q}",), chain=f"A:xsl{tg}")
        chain_tail(P, cx, "sync", xsl[:, 0:32], f"A:xsl{tg}")
        xkeys = tuple(f"A:xsl{xs}{q}" for q in range(4))

        for wi, (wt, wck, bias, dst) in enumerate(
                ((io["wq_b"], "wc:wq_b", cx.bq, io["q_s"]),
                 (io["wk_b"], "wc:wk_b", cx.bk, io["k_s"]))):
            for mo in range(16):
                sl = (tg * 32 + wi * 16 + mo) % 3
                P.dma("scalar",
                      lambda e, wt=wt, mo=mo, sl=sl: e.dma_start(
                          ws[sl][:, :].rearrange("p (k j) -> p k j", j=128),
                          wt[mo]),
                      r=(wck,), w=(f"A:ws{sl}",), chain="A:ws")
                pk = 4 + (mo % 2)
                for ki in range(16):
                    P.t(lambda e, ki=ki, sl=sl, pk=pk, xsl=xsl: e.matmul(
                        cx.ps[pk][:, :], ws[sl][:, ki*128:(ki+1)*128],
                        xsl[:, ki*512:(ki+1)*512],
                        start=(ki == 0), stop=(ki == 15)),
                        r=(f"A:ws{sl}",) + xkeys, w=(f"ps{pk}",))
                osl = mo % 4
                P.v(lambda e, pk=pk, osl=osl, bias=bias, wi=wi, mo=mo:
                    e.tensor_scalar(ob[osl][:, :], cx.ps[pk][:, :],
                                    bias[:, mo:mo+1], None, ALU.add),
                    r=(f"ps{pk}", "c:bq", "c:bk"), w=(f"A:ob{osl}",))
                P.dma("gpsimd",
                      lambda e, dst=dst, mo=mo, c0=c0, osl=osl: e.dma_start(
                          dst[mo*128:(mo+1)*128, c0:c0+512], ob[osl][:, :]),
                      r=(f"A:ob{osl}",), w=())

        for half in range(2):
            for ki in range(16):
                sl = (half * 16 + ki) % 4
                P.dma("sync",
                      lambda e, ki=ki, half=half, sl=sl: e.dma_start(
                          wv[sl][:, :],
                          io["wv_b"][ki, :, half*1024:(half+1)*1024]),
                      r=("wc:wv_b",), w=(f"A:wv{sl}",), chain="A:wv")
                for tc in range(4):
                    for ng in range(2):
                        pk = tc * 2 + ng
                        P.t(lambda e, ki=ki, tc=tc, ng=ng, sl=sl, pk=pk,
                            xsl=xsl: e.matmul(
                                cx.ps[pk][:, :],
                                xsl[:, ki*512+tc*128:ki*512+(tc+1)*128],
                                wv[sl][:, ng*512:(ng+1)*512],
                                start=(ki == 0), stop=(ki == 15)),
                            r=(f"A:wv{sl}",) + xkeys, w=(f"ps{pk}",))
            for tc in range(4):
                tch = tg * 4 + tc
                for ng in range(2):
                    pk = tc * 2 + ng
                    osl = pk % 4
                    P.v(lambda e, pk=pk, osl=osl: e.tensor_copy(
                        ob[osl][:, :], cx.ps[pk][:, :]),
                        r=(f"ps{pk}",), w=(f"A:ob{osl}",))
                    dv0 = half * 1024 + ng * 512
                    P.dma("gpsimd",
                          lambda e, tch=tch, dv0=dv0, osl=osl: e.dma_start(
                              io["v_s"][tch, :, dv0:dv0+512], ob[osl][:, :]),
                          r=(f"A:ob{osl}",), w=())


def emit_phaseB(P, cx, stack):
    nc, io = cx.nc, cx.io
    sb = lambda n, s_, d: stack.enter_context(nc.sbuf_tensor(n, s_, d))
    qh = [sb(f"B_qh{i}", [128, T], BF16) for i in range(3)]
    kh = [sb(f"B_kh{i}", [128, T], BF16) for i in range(3)]
    vb = [sb(f"B_vb{i}", [128, T], BF16) for i in range(3)]
    sq = [sb(f"B_sq{i}", [128, T], BF16) for i in range(2)]
    nbuf = [sb(f"B_nbuf{i}", [1, 2 * T], F32) for i in range(2)]
    grow = sb("B_grow", [1, 64], F32)
    cells = sb("B_cells", [1, 8], F32)
    cellb = sb("B_cellb", [1, 1], BF16)
    negbc = sb("B_negbc", [128, 1], F32)
    eb = [sb(f"B_eb{i}", [128, 1024], BF16) for i in range(2)]
    rec = [sb(f"B_rec{i}", [1, 512], F32) for i in range(2)]
    recb = [sb(f"B_recb{i}", [1, 512], BF16) for i in range(2)]
    sbb = [sb(f"B_sbb{i}", [128, 512], F32) for i in range(2)]
    ab = [sb(f"B_ab{i}", [128, 512], BF16) for i in range(2)]
    pairs = [(b, h) for b in range(NB) for h in range(H)]

    def emit_loads(p, qk_only=False):
        b, h = pairs[p]
        r0, c0, sl = h * 128, b * T, p % 3
        P.dma("sync", lambda e, r0=r0, c0=c0, sl=sl: e.dma_start(
            qh[sl][:, :], io["q_s"][r0:r0+128, c0:c0+T]), w=(f"B:qh{sl}",))
        P.dma("sync", lambda e, r0=r0, c0=c0, sl=sl: e.dma_start(
            kh[sl][:, :], io["k_s"][r0:r0+128, c0:c0+T]), w=(f"B:kh{sl}",))
        if qk_only:
            return
        P.dma("scalar", lambda e, b=b, h=h, sl=sl: e.dma_start(
            vb[sl][:, :].rearrange("p (ti j) -> p ti j", j=128),
            io["v_s"][b*16:(b+1)*16, :, h*128:(h+1)*128].rearrange(
                "ti p j -> p ti j")),
            w=(f"B:vb{sl}",))

    emit_loads(0, qk_only=True)
    for p in range(len(pairs)):
        sl, l2 = p % 3, p % 2
        if p + 1 < len(pairs):
            emit_loads(p + 1, qk_only=True)
        for src, key, off in ((qh[sl], f"B:qh{sl}", 0),
                              (kh[sl], f"B:kh{sl}", T)):
            P.v(lambda e, src=src, l2=l2: e.tensor_mul(
                sq[l2][:, :], src[:, :], src[:, :]),
                r=(key,), w=(f"B:sq{l2}",))
            for g4 in range(4):
                pk = 6 + g4 % 2
                P.t(lambda e, g4=g4, l2=l2, pk=pk: e.matmul(
                    cx.ps[pk][0:1, :], cx.ones_col[:, :],
                    sq[l2][:, g4*512:(g4+1)*512], start=True, stop=True),
                    r=(f"B:sq{l2}", "c:ones_col"), w=(f"ps{pk}",))
                P.v(lambda e, g4=g4, off=off, l2=l2, pk=pk: e.tensor_copy(
                    nbuf[l2][0:1, off+g4*512:off+(g4+1)*512],
                    cx.ps[pk][0:1, :]),
                    r=(f"ps{pk}",), w=(f"B:nbuf{l2}",))
        P.v(lambda e, l2=l2, p=p: e.reduce_max(grow[0:1, p:p+1],
                                               nbuf[l2][0:1, 0:T], AX.X),
            r=(f"B:nbuf{l2}",), w=("B:grow",))
        P.v(lambda e, l2=l2, p=p: e.reduce_max(grow[0:1, 32+p:33+p],
                                               nbuf[l2][0:1, T:2*T], AX.X),
            r=(f"B:nbuf{l2}",), w=("B:grow",))
    P.v(lambda e: e.tensor_copy(sq[0][:, :], sq[0][:, :]), w=("B:pad",))
    P.v(lambda e: e.tensor_copy(sq[1][:, :], sq[1][:, :]), w=("B:pad",))
    P.v(lambda e: e.reduce_max(cells[0:1, 0:1], grow[0:1, 0:32], AX.X),
        r=("B:grow", "B:pad"), w=("B:c0",))
    P.v(lambda e: e.reduce_max(cells[0:1, 1:2], grow[0:1, 32:64], AX.X),
        r=("B:grow",), w=("B:c1",))
    P.g(lambda e: e.tensor_add(cells[0:1, 2:3], cells[0:1, 0:1],
                               cells[0:1, 1:2]),
        r=("B:c0", "B:c1"), w=("B:c2",))
    P.v(lambda e: e.tensor_scalar_mul(cells[0:1, 3:4], cells[0:1, 2:3],
                                      -SCALE_ATT / 2.0),
        r=("B:c2",), w=("B:c3",))
    P.g(lambda e: e.tensor_copy(cellb[0:1, 0:1], cells[0:1, 3:4]),
        r=("B:c3",), w=("B:cellb",))
    P.t(lambda e: e.matmul(cx.ps[6][:, 0:1], cx.ones_row[:, :],
                           cellb[0:1, 0:1], start=True, stop=True),
        r=("B:cellb", "c:ones_row"), w=("ps6",))
    P.v(lambda e: e.tensor_copy(negbc[:, 0:1], cx.ps[6][:, 0:1]),
        r=("ps6",), w=("B:negbc",))

    def emit_main(p):
        b, h = pairs[p]
        r0, c0, sl = h * 128, b * T, p % 3

        def s_duo(g, q, nduo):
            dl = q % 2
            pan = "ps01" if dl == 0 else "ps23"
            dq = q - (nduo - 2)
            for i in range(2):
                j = 2 * q + i
                diag = dq >= 0
                P.t(lambda e, j=j, g=g, dl=dl, i=i, sl=sl, diag=diag:
                    e.matmul(
                        cx.psD[dl][:, i*512:(i+1)*512],
                        kh[sl][:, j*128:(j+1)*128],
                        qh[sl][:, g*512:(g+1)*512], start=True,
                        stop=not diag),
                    r=(f"B:kh{sl}", f"B:qh{sl}"), w=(pan,))
                if diag:
                    pat = dq * 2 + i
                    P.t(lambda e, dl=dl, i=i, pat=pat: e.matmul(
                        cx.psD[dl][:, i*512:(i+1)*512], cx.identb[:, :],
                        cx.masks[:, pat*512:(pat+1)*512],
                        start=False, stop=True),
                        r=("c:identb", "c:masks"), w=(pan,))

        def exp_duo(g, q, nduo):
            dl = q % 2
            pan = "ps01" if dl == 0 else "ps23"
            P.s(lambda e, dl=dl: e.activation(
                eb[dl][:, :], cx.psD[dl][:, :], AF.Exp,
                bias=negbc[:, 0:1], scale=SCALE_ATT),
                r=(pan, "B:negbc"), w=(f"B:eb{dl}",))

        def denpv_duo(g, q, nduo):
            dl = q % 2
            gp = g % 2
            pv, dn = 4 + gp, 6 + gp
            for i in range(2):
                j = 2 * q + i
                P.t(lambda e, dl=dl, i=i, q=q, nduo=nduo, dn=dn: e.matmul(
                    cx.ps[dn][0:1, :], cx.ones_col[:, :],
                    eb[dl][:, i*512:(i+1)*512],
                    start=(q == 0 and i == 0),
                    stop=(q == nduo - 1 and i == 1)),
                    r=(f"B:eb{dl}", "c:ones_col"), w=(f"ps{dn}",))
                P.t(lambda e, dl=dl, i=i, j=j, q=q, nduo=nduo, sl=sl, pv=pv:
                    e.matmul(
                        cx.ps[pv][:, :], vb[sl][:, j*128:(j+1)*128],
                        eb[dl][:, i*512:(i+1)*512],
                        start=(q == 0 and i == 0),
                        stop=(q == nduo - 1 and i == 1)),
                    r=(f"B:eb{dl}", f"B:vb{sl}"), w=(f"ps{pv}",))

        for g in range(4):
            nduo = 2 * g + 2
            gp = g % 2
            pv, dn = 4 + gp, 6 + gp
            s_duo(g, 0, nduo)
            for q in range(nduo):
                exp_duo(g, q, nduo)
                if q + 1 < nduo:
                    s_duo(g, q + 1, nduo)
                denpv_duo(g, q, nduo)
            P.v(lambda e, gp=gp, dn=dn: e.reciprocal(rec[gp][0:1, :],
                                                     cx.ps[dn][0:1, :]),
                r=(f"ps{dn}",), w=(f"B:rec{gp}",))
            P.v(lambda e, gp=gp: e.tensor_copy(recb[gp][0:1, :],
                                               rec[gp][0:1, :]),
                r=(f"B:rec{gp}",), w=(f"B:recb{gp}",))
            P.t(lambda e, gp=gp, dn=dn: e.matmul(
                cx.ps[dn][:, :], cx.ones_row[:, :], recb[gp][0:1, :],
                start=True, stop=True),
                r=(f"B:recb{gp}", "c:ones_row"), w=(f"ps{dn}",))
            P.v(lambda e, gp=gp, dn=dn: e.tensor_copy(sbb[gp][:, :],
                                                      cx.ps[dn][:, :]),
                r=(f"ps{dn}",), w=(f"B:sbb{gp}",))
            P.v(lambda e, gp=gp, pv=pv: e.tensor_mul(
                ab[gp][:, :], cx.ps[pv][:, :], sbb[gp][:, :]),
                r=(f"ps{pv}", f"B:sbb{gp}"), w=(f"B:ab{gp}",))
            P.dma("sync", lambda e, r0=r0, c0=c0, g=g, gp=gp: e.dma_start(
                io["attn_s"][r0:r0+128, c0+g*512:c0+(g+1)*512],
                ab[gp][:, :]),
                r=(f"B:ab{gp}",), w=())

    emit_loads(0)
    emit_loads(1)
    for p in range(len(pairs)):
        if p + 2 < len(pairs):
            emit_loads(p + 2)
        emit_main(p)


class LNKit:
    def __init__(self, P, cx, stack, pfx, out_f32=False):
        nc = cx.nc
        self.P, self.cx, self.pfx = P, cx, pfx
        sb = lambda n, s, d: stack.enter_context(nc.sbuf_tensor(pfx + n, s, d))
        self.xr = sb("_xr", [128, NKI * 512], F32)
        self.xrb = sb("_xrb", [128, NKI * 512], BF16)
        self.sqb = sb("_sqb", [128, NKI * 512], BF16)
        self.yb = sb("_yb", [128, NKI * 512], F32 if out_f32 else BF16)
        row = lambda n: sb(n, [1, 512], F32)
        self.mbuf, self.e2buf, self.m2buf = row("_m"), row("_e2"), row("_m2")
        self.varbuf, self.sdbuf = row("_var"), row("_sd")
        self.abuf, self.bbuf = row("_a"), row("_b")
        self.abb = sb("_abb", [1, 512], BF16)
        self.bbb = sb("_bbb", [1, 512], BF16)
        self.out_f32 = out_f32

    def emit(self, c0, dst_r):
        P, cx, pfx = self.P, self.cx, self.pfx
        xr, xrb, sqb, yb = self.xr, self.xrb, self.sqb, self.yb
        for mo in range(16):
            P.v(lambda e, mo=mo: e.tensor_copy(
                xrb[:, mo*512:(mo+1)*512], xr[:, mo*512:(mo+1)*512]),
                r=(f"{pfx}:xr{mo}",), w=(f"{pfx}:xrb{mo}",))
            P.s(lambda e, mo=mo: e.activation(
                sqb[:, mo*512:(mo+1)*512], xr[:, mo*512:(mo+1)*512],
                AF.Square),
                r=(f"{pfx}:xr{mo}",), w=(f"{pfx}:sqb{mo}",))
        for mo in range(16):
            P.t(lambda e, mo=mo: e.matmul(
                cx.ps[6][0:1, :], cx.ones_col[:, :],
                xrb[:, mo*512:(mo+1)*512],
                start=(mo == 0), stop=(mo == 15)),
                r=(f"{pfx}:xrb{mo}", "c:ones_col"), w=("ps6",))
            P.t(lambda e, mo=mo: e.matmul(
                cx.ps[7][0:1, :], cx.ones_col[:, :],
                sqb[:, mo*512:(mo+1)*512],
                start=(mo == 0), stop=(mo == 15)),
                r=(f"{pfx}:sqb{mo}", "c:ones_col"), w=("ps7",))
        P.v(lambda e: e.tensor_scalar_mul(self.mbuf[0:1, :], cx.ps[6][0:1, :],
                                          1.0 / D),
            r=("ps6",), w=(f"{pfx}:m",))
        P.v(lambda e: e.tensor_scalar_mul(self.e2buf[0:1, :], cx.ps[7][0:1, :],
                                          1.0 / D),
            r=("ps7",), w=(f"{pfx}:e2",))
        P.v(lambda e: e.tensor_mul(self.m2buf[0:1, :], self.mbuf[0:1, :],
                                   self.mbuf[0:1, :]),
            r=(f"{pfx}:m",), w=(f"{pfx}:m2",))
        P.v(lambda e: e.tensor_sub(self.varbuf[0:1, :], self.e2buf[0:1, :],
                                   self.m2buf[0:1, :]),
            r=(f"{pfx}:e2", f"{pfx}:m2"), w=(f"{pfx}:var",))
        P.v(lambda e: e.tensor_scalar_add(self.varbuf[0:1, :],
                                          self.varbuf[0:1, :], 1e-5),
            r=(f"{pfx}:var",), w=(f"{pfx}:var",))
        P.s(lambda e: e.activation(self.sdbuf[0:1, :], self.varbuf[0:1, :],
                                   AF.Sqrt),
            r=(f"{pfx}:var",), w=(f"{pfx}:sd",))
        P.v(lambda e: e.reciprocal(self.abuf[0:1, :], self.sdbuf[0:1, :]),
            r=(f"{pfx}:sd",), w=(f"{pfx}:a",))
        P.v(lambda e: e.scalar_tensor_tensor(self.bbuf[0:1, :],
                                             self.mbuf[0:1, :], -1.0,
                                             self.abuf[0:1, :],
                                             ALU.mult, ALU.mult),
            r=(f"{pfx}:m", f"{pfx}:a"), w=(f"{pfx}:b",))
        P.v(lambda e: e.tensor_copy(self.abb[0:1, :], self.abuf[0:1, :]),
            r=(f"{pfx}:a",), w=(f"{pfx}:abb",))
        P.v(lambda e: e.tensor_copy(self.bbb[0:1, :], self.bbuf[0:1, :]),
            r=(f"{pfx}:b",), w=(f"{pfx}:bbb",))
        P.t(lambda e: e.matmul(cx.ps[4][:, :], cx.ones_row[:, :],
                               self.abb[0:1, :], start=True, stop=True),
            r=(f"{pfx}:abb", "c:ones_row"), w=("ps4",))
        P.t(lambda e: e.matmul(cx.ps[5][:, :], cx.ones_row[:, :],
                               self.bbb[0:1, :], start=True, stop=True),
            r=(f"{pfx}:bbb", "c:ones_row"), w=("ps5",))
        for mo in range(16):
            P.v(lambda e, mo=mo: e.tensor_mul(
                xrb[:, mo*512:(mo+1)*512], xr[:, mo*512:(mo+1)*512],
                cx.ps[4][:, :]),
                r=(f"{pfx}:xr{mo}", "ps4"), w=(f"{pfx}:xrb{mo}",))
            P.v(lambda e, mo=mo: e.tensor_add(
                yb[:, mo*512:(mo+1)*512], xrb[:, mo*512:(mo+1)*512],
                cx.ps[5][:, :]),
                r=(f"{pfx}:xrb{mo}", "ps5"), w=(f"{pfx}:yb{mo}",))
        for q in range(4):
            P.dma("sync", lambda e, q=q, c0=c0: e.dma_start(
                dst_r[:, 4*q:4*q+4, c0:c0+512],
                yb[:, :].rearrange("p (k t) -> p k t", t=512)[:, 4*q:4*q+4, :]),
                r=tuple(f"{self.pfx}:yb{mo}" for mo in range(4*q, 4*q+4)), w=())


def emit_out_proj_ln(P, cx, stack, w_t, src, resid, dst, use_bias):
    w_b = {"wo_t": "wo_b", "l1wo_t": "l1wo_b"}[w_t]
    nc, io = cx.nc, cx.io
    pfx = dst
    sb = lambda n, s, d: stack.enter_context(nc.sbuf_tensor(pfx + n, s, d))
    asl = sb("_asl", [128, NKI * 512], BF16)
    rsl = sb("_rsl", [128, NKI * 512], F32)
    ws = [sb(f"_ws{i}", [128, NKI * 128], BF16) for i in range(3)]
    nw = [0]
    biasc = sb("_biasc", [128, 16], F32)
    ln = LNKit(P, cx, stack, pfx)
    xr = ln.xr
    wt = io[w_b]
    wck = f"wc:{w_b}"
    src_r = io[src].rearrange("(k p) t -> p k t", p=128)
    res_r = io[resid].rearrange("(k p) t -> p k t", p=128)
    dst_r = io[dst].rearrange("(k p) t -> p k t", p=128)

    if use_bias:
        for mo in range(16):
            sl = nw[0] % 3
            nw[0] += 1
            P.dma("scalar", lambda e, mo=mo, sl=sl: e.dma_start(
                ws[sl][:, :].rearrange("p (k j) -> p k j", j=128), wt[mo]),
                r=(wck,), w=(f"{pfx}:ws{sl}",), chain=f"{pfx}:ws")
            chain_tail(P, cx, "scalar", ws[sl][:, 0:32], f"{pfx}:ws")
            for ki in range(16):
                P.t(lambda e, ki=ki, sl=sl: e.matmul(
                    cx.ps[6][:, 0:1], ws[sl][:, ki*128:(ki+1)*128],
                    cx.bvb[:, ki:ki+1], start=(ki == 0), stop=(ki == 15)),
                    r=(f"{pfx}:ws{sl}", "c:bvb"), w=("ps6",))
            P.v(lambda e, mo=mo: e.tensor_scalar(
                biasc[:, mo:mo+1], cx.ps[6][:, 0:1], cx.bo[:, mo:mo+1],
                None, ALU.add),
                r=("ps6", "c:bo"), w=(f"{pfx}:biasc",))

    for tg in range(8):
        c0 = tg * 512
        for q in range(4):
            P.dma("sync", lambda e, q=q, c0=c0: e.dma_start(
                asl[:, :].rearrange("p (k t) -> p k t", t=512)[:, 4*q:4*q+4, :],
                src_r[:, 4*q:4*q+4, c0:c0+512]),
                w=(f"{pfx}:asl{q}",))
            P.dma("gpsimd", lambda e, q=q, c0=c0: e.dma_start(
                rsl[:, :].rearrange("p (k t) -> p k t", t=512)[:, 4*q:4*q+4, :],
                res_r[:, 4*q:4*q+4, c0:c0+512]),
                w=(f"{pfx}:rsl{q}",))
        akeys = tuple(f"{pfx}:asl{q}" for q in range(4))
        rkeys = tuple(f"{pfx}:rsl{q}" for q in range(4))
        for mo in range(16):
            sl = nw[0] % 3
            nw[0] += 1
            pk = mo % 2
            P.dma("scalar", lambda e, mo=mo, sl=sl: e.dma_start(
                ws[sl][:, :].rearrange("p (k j) -> p k j", j=128), wt[mo]),
                r=(wck,), w=(f"{pfx}:ws{sl}",), chain=f"{pfx}:ws")
            chain_tail(P, cx, "scalar", ws[sl][:, 0:32], f"{pfx}:ws")
            for ki in range(16):
                P.t(lambda e, ki=ki, sl=sl, pk=pk: e.matmul(
                    cx.ps[pk][:, :], ws[sl][:, ki*128:(ki+1)*128],
                    asl[:, ki*512:(ki+1)*512],
                    start=(ki == 0), stop=(ki == 15)),
                    r=(f"{pfx}:ws{sl}",) + akeys, w=(f"ps{pk}",))
            if use_bias:
                P.v(lambda e, mo=mo, pk=pk: e.scalar_tensor_tensor(
                    xr[:, mo*512:(mo+1)*512], cx.ps[pk][:, :],
                    biasc[:, mo:mo+1], rsl[:, mo*512:(mo+1)*512],
                    ALU.add, ALU.add),
                    r=(f"ps{pk}", f"{pfx}:biasc") + rkeys, w=(f"{pfx}:xr{mo}",))
            else:
                P.v(lambda e, mo=mo, pk=pk: e.tensor_add(
                    xr[:, mo*512:(mo+1)*512], cx.ps[pk][:, :],
                    rsl[:, mo*512:(mo+1)*512]),
                    r=(f"ps{pk}",) + rkeys, w=(f"{pfx}:xr{mo}",))
        ln.emit(c0, dst_r)


def emit_phaseD1(P, cx, stack):
    nc, io = cx.nc, cx.io
    sb = lambda n, s, d: stack.enter_context(nc.sbuf_tensor(n, s, d))
    ybig = sb("D1_ybig", [128, NKI * NT], BF16)
    wg = [sb(f"D1_wg{i}", [128, NKI * 128], BF16) for i in range(3)]
    wu = [sb(f"D1_wu{i}", [128, NKI * 128], BF16) for i in range(3)]
    sigb = sb("D1_sig", [128, 512], BF16)
    gbuf = sb("D1_g", [128, 512], F32)
    ubuf = sb("D1_u", [128, 512], BF16)
    gsb = sb("D1_gs", [128, 512], BF16)
    actb = [sb(f"D1_act{i}", [128, 512], BF16) for i in range(2)]
    y_r = io["y1_s"].rearrange("(k p) t -> p k t", p=128)
    a_r = io["act_s"].rearrange("(k p) t -> p k t", p=128)
    for ki in range(16):
        P.dma("sync", lambda e, ki=ki: e.dma_start(
            ybig[:, ki*NT:(ki+1)*NT], y_r[:, ki, :]), w=(f"D1:y{ki}",))
    ykeys = tuple(f"D1:y{ki}" for ki in range(16))
    for m in range(64):
        sl = m % 3
        P.dma("gpsimd", lambda e, m=m, sl=sl: e.dma_start(
            wg[sl][:, :].rearrange("p (k j) -> p k j", j=128), io["wg_t"][m]),
            w=(f"D1:wg{sl}",), chain="D1:wg")
        P.dma("gpsimd", lambda e, m=m, sl=sl: e.dma_start(
            wu[sl][:, :].rearrange("p (k j) -> p k j", j=128), io["wu_t"][m]),
            w=(f"D1:wu{sl}",), chain="D1:wu")
        for tg in range(8):
            pg = tg % 2
            pu = 2 + tg % 2
            for ki in range(16):
                P.t(lambda e, ki=ki, sl=sl, tg=tg, pg=pg: e.matmul(
                    cx.ps[pg][:, :], wg[sl][:, ki*128:(ki+1)*128],
                    ybig[:, ki*NT+tg*512:ki*NT+(tg+1)*512],
                    start=(ki == 0), stop=(ki == 15)),
                    r=(f"D1:wg{sl}",) + ykeys, w=(f"ps{pg}",))
            for ki in range(16):
                P.t(lambda e, ki=ki, sl=sl, tg=tg, pu=pu: e.matmul(
                    cx.ps[pu][:, :], wu[sl][:, ki*128:(ki+1)*128],
                    ybig[:, ki*NT+tg*512:ki*NT+(tg+1)*512],
                    start=(ki == 0), stop=(ki == 15)),
                    r=(f"D1:wu{sl}",) + ykeys, w=(f"ps{pu}",))
            P.s(lambda e, m=m, pg=pg: e.activation(
                sigb[:, :], cx.ps[pg][:, :], AF.Sigmoid,
                scale=cx.sg[:, m:m+1]),
                r=(f"ps{pg}", "c:sg"), w=("D1:sig",))
            P.v(lambda e, m=m, pg=pg: e.tensor_scalar_mul(
                gbuf[:, :], cx.ps[pg][:, :], cx.sg[:, m:m+1]),
                r=(f"ps{pg}", "c:sg"), w=("D1:g",))
            P.v(lambda e, m=m, pu=pu: e.tensor_scalar_mul(
                ubuf[:, :], cx.ps[pu][:, :], cx.su[:, m:m+1]),
                r=(f"ps{pu}", "c:su"), w=("D1:u",))
            P.v(lambda e: e.tensor_mul(gsb[:, :], gbuf[:, :], sigb[:, :]),
                r=("D1:g", "D1:sig"), w=("D1:gs",))
            asl_ = tg % 2
            P.v(lambda e, asl_=asl_: e.tensor_mul(
                actb[asl_][:, :], gsb[:, :], ubuf[:, :]),
                r=("D1:gs", "D1:u"), w=(f"D1:act{asl_}",))
            P.dma("sync", lambda e, m=m, tg=tg, asl_=asl_: e.dma_start(
                a_r[:, m, tg*512:(tg+1)*512], actb[asl_][:, :]),
                r=(f"D1:act{asl_}",), w=())


def emit_phaseD3(P, cx, stack):
    nc, io = cx.nc, cx.io
    sb = lambda n, s, d: stack.enter_context(nc.sbuf_tensor(n, s, d))
    abig = sb("D3_abig", [128, 64 * 512], BF16)
    rsl = sb("D3_rsl", [128, NKI * 512], BF16)
    wd = sb("D3_wd", [128, 3 * 32 * 128], BF16)
    ln = LNKit(P, cx, stack, "D3")
    xr = ln.xr
    a_r = io["act_s"].rearrange("(k p) t -> p k t", p=128)
    y_r = io["y1_s"].rearrange("(k p) t -> p k t", p=128)
    dst_r = io["y2_s"].rearrange("(k p) t -> p k t", p=128)
    for tg in range(8):
        c0 = tg * 512
        for q in range(8):
            P.dma("sync", lambda e, q=q, c0=c0: e.dma_start(
                abig[:, :].rearrange("p (k t) -> p k t", t=512)[:, 8*q:8*q+8, :],
                a_r[:, 8*q:8*q+8, c0:c0+512]),
                w=(f"D3:a{q}",))
        for q in range(4):
            P.dma("sync", lambda e, q=q, c0=c0: e.dma_start(
                rsl[:, :].rearrange("p (k t) -> p k t", t=512)[:, 4*q:4*q+4, :],
                y_r[:, 4*q:4*q+4, c0:c0+512]),
                w=(f"D3:r{q}",))
        akeys = tuple(f"D3:a{q}" for q in range(8))
        rkeys = tuple(f"D3:r{q}" for q in range(4))
        for mo in range(16):
            pk = mo % 2
            for hf in range(2):
                hs = (mo * 2 + hf) % 3
                P.dma("scalar",
                      lambda e, mo=mo, hf=hf, hs=hs: e.dma_start(
                          wd[:, hs*4096:(hs+1)*4096].rearrange(
                              "p (k j) -> p k j", j=128),
                          io["wd_b"][mo, :, hf*32:(hf+1)*32, :]),
                      r=("wc:wd_b",), w=(f"D3:wd{hs}",), chain=f"D3:wd{tg}")
            for ki in range(64):
                hf = ki // 32
                hs = (mo * 2 + hf) % 3
                lk = ki % 32
                P.t(lambda e, lk=lk, ki=ki, pk=pk, hs=hs: e.matmul(
                    cx.ps[pk][:, :], wd[:, hs*4096+lk*128:hs*4096+(lk+1)*128],
                    abig[:, ki*512:(ki+1)*512],
                    start=(ki == 0), stop=(ki == 63)),
                    r=(f"D3:wd{hs}",) + akeys, w=(f"ps{pk}",))
            P.v(lambda e, mo=mo, pk=pk: e.scalar_tensor_tensor(
                xr[:, mo*512:(mo+1)*512], cx.ps[pk][:, :],
                cx.sd[:, mo:mo+1], rsl[:, mo*512:(mo+1)*512],
                ALU.mult, ALU.add),
                r=(f"ps{pk}", "c:sd") + rkeys, w=(f"D3:xr{mo}",))
        chain_tail(P, cx, "scalar", wd[:, 0:32], f"D3:wd{tg}")
        ln.emit(c0, dst_r)


def emit_phaseE1(P, cx, stack):
    nc, io = cx.nc, cx.io
    sb = lambda n, s, d: stack.enter_context(nc.sbuf_tensor(n, s, d))
    ybig = sb("E1_ybig", [128, NKI * NT], BF16)
    ws = [sb(f"E1_ws{i}", [128, NKI * 128], BF16) for i in range(3)]
    ob = [sb(f"E1_ob{i}", [128, 512], BF16) for i in range(4)]
    y_r = io["y2_s"].rearrange("(k p) t -> p k t", p=128)
    for ki in range(16):
        P.dma("sync", lambda e, ki=ki: e.dma_start(
            ybig[:, ki*NT:(ki+1)*NT], y_r[:, ki, :]), w=(f"E1:y{ki}",))
    ykeys = tuple(f"E1:y{ki}" for ki in range(16))
    for wi, (wname, dname) in enumerate((("l1wq_b", "q2_s"),
                                         ("l1wkv_b", "kv2_s"))):
        wt = io[wname]
        wck = f"wc:{wname}"
        dst = io[dname]
        for mo in range(16):
            sl = (wi * 16 + mo) % 3
            P.dma("scalar", lambda e, wt=wt, mo=mo, sl=sl: e.dma_start(
                ws[sl][:, :].rearrange("p (k j) -> p k j", j=128), wt[mo]),
                r=(wck,), w=(f"E1:ws{sl}",), chain="E1:ws")
            for tg in range(8):
                pk = tg % 2
                for ki in range(16):
                    P.t(lambda e, ki=ki, sl=sl, tg=tg, pk=pk: e.matmul(
                        cx.ps[pk][:, :], ws[sl][:, ki*128:(ki+1)*128],
                        ybig[:, ki*NT+tg*512:ki*NT+(tg+1)*512],
                        start=(ki == 0), stop=(ki == 15)),
                        r=(f"E1:ws{sl}",) + ykeys, w=(f"ps{pk}",))
                osl = tg % 4
                P.v(lambda e, pk=pk, osl=osl: e.tensor_copy(
                    ob[osl][:, :], cx.ps[pk][:, :]),
                    r=(f"ps{pk}",), w=(f"E1:ob{osl}",))
                P.dma("sync",
                      lambda e, dst=dst, mo=mo, tg=tg, osl=osl: e.dma_start(
                          dst[mo*128:(mo+1)*128, tg*512:(tg+1)*512],
                          ob[osl][:, :]),
                      r=(f"E1:ob{osl}",), w=())


def emit_phaseE2(P, cx, stack):
    nc, io = cx.nc, cx.io
    sb = lambda n, s_, d: stack.enter_context(nc.sbuf_tensor(n, s_, d))
    qh = [sb(f"E2_qh{i}", [128, T], BF16) for i in range(3)]
    kvh = [sb(f"E2_kvh{i}", [128, T], BF16) for i in range(3)]
    sq2 = [sb(f"E2_sq{i}", [128, T], BF16) for i in range(2)]
    nrow = [sb(f"E2_nrow{i}", [1, T], F32) for i in range(2)]
    nrowb = [sb(f"E2_nrowb{i}", [1, T], BF16) for i in range(2)]
    sdrow = [sb(f"E2_sdrow{i}", [1, T], F32) for i in range(2)]
    qhat = [sb(f"E2_qhat{i}", [128, T], BF16) for i in range(2)]
    kcs = [sb(f"E2_kcs{i}", [128, T], BF16) for i in range(2)]
    khat = sb("E2_khat", [128, T], BF16)
    zerosb = sb("E2_zeros", [128, T], BF16)
    qfre = sb("E2_qfre", [128, T], BF16)
    qfim = sb("E2_qfim", [128, T], BF16)
    cfre = sb("E2_cfre", [128, T], BF16)
    cfim = sb("E2_cfim", [128, T], BF16)
    qvre = sb("E2_qvre", [128, T], BF16)
    qvim = sb("E2_qvim", [128, T], BF16)
    tm1 = sb("E2_tm1", [128, T], BF16)
    tm2 = sb("E2_tm2", [128, T], BF16)
    tm3 = sb("E2_tm3", [128, T], BF16)
    tm4 = sb("E2_tm4", [128, T], BF16)
    vhb = sb("E2_vhb", [128, T], BF16)
    bcb = [sb(f"E2_bcb{i}", [128, 512], BF16) for i in range(2)]
    P.v(lambda e: e.memset(zerosb[:, :], 0.0), w=("E2:zeros",))
    pairs = [(b, h) for b in range(NB) for h in range(H)]

    def emit_loads(p):
        b, h = pairs[p]
        r0, c0, sl = h * 128, b * T, p % 3
        P.dma("sync", lambda e, r0=r0, c0=c0, sl=sl: e.dma_start(
            qh[sl][:, :], io["q2_s"][r0:r0+128, c0:c0+T]), w=(f"E2:qh{sl}",))
        P.dma("sync", lambda e, r0=r0, c0=c0, sl=sl: e.dma_start(
            kvh[sl][:, :], io["kv2_s"][r0:r0+128, c0:c0+T]), w=(f"E2:kvh{sl}",))

    def stage_a(p):
        sl, l2 = p % 3, p % 2
        for src, skey, dstb, dkey in (
                (qh[sl], f"E2:qh{sl}", qhat[l2], f"E2:qhat{l2}"),
                (kvh[sl], f"E2:kvh{sl}", khat, "E2:khat")):
            P.v(lambda e, src=src, l2=l2: e.tensor_mul(
                sq2[l2][:, :], src[:, :], src[:, :]),
                r=(skey,), w=(f"E2:sq{l2}",))
            for g in range(4):
                P.t(lambda e, g=g, l2=l2: e.matmul(
                    cx.ps[5][0:1, :], cx.ones_col[:, :],
                    sq2[l2][:, g*512:(g+1)*512], start=True, stop=True),
                    r=(f"E2:sq{l2}", "c:ones_col"), w=("ps5",))
                P.s(lambda e, g=g, l2=l2: e.activation(
                    sdrow[l2][0:1, g*512:(g+1)*512], cx.ps[5][0:1, :],
                    AF.Sqrt, scale=128.0),
                    r=("ps5",), w=(f"E2:sd{l2}",))
            P.v(lambda e, l2=l2: e.reciprocal(nrow[l2][0:1, :],
                                              sdrow[l2][0:1, :]),
                r=(f"E2:sd{l2}",), w=(f"E2:nrow{l2}",))
            P.v(lambda e, l2=l2: e.tensor_copy(nrowb[l2][0:1, :],
                                               nrow[l2][0:1, :]),
                r=(f"E2:nrow{l2}",), w=(f"E2:nrowb{l2}",))
            for g in range(4):
                bs = g % 2
                P.t(lambda e, g=g, l2=l2: e.matmul(
                    cx.ps[6][:, :], cx.ones_row[:, :],
                    nrowb[l2][0:1, g*512:(g+1)*512], start=True, stop=True),
                    r=(f"E2:nrowb{l2}", "c:ones_row"), w=("ps6",))
                P.s(lambda e, bs=bs: e.copy(bcb[bs][:, :], cx.ps[6][:, :]),
                    r=("ps6",), w=(f"E2:bcb{bs}",))
                emit = P.v if g % 2 == 0 else P.g
                emit(lambda e, g=g, src=src, dstb=dstb, bs=bs: e.tensor_mul(
                    dstb[:, g*512:(g+1)*512], src[:, g*512:(g+1)*512],
                    bcb[bs][:, :]),
                    r=(skey, f"E2:bcb{bs}"), w=(dkey,))
        P.v(lambda e, l2=l2: e.tensor_tensor_scan(
            kcs[l2][:, :], khat[:, :], zerosb[:, :], 0.0, ALU.add, ALU.add),
            r=("E2:khat", "E2:zeros"), w=(f"E2:kcs{l2}",))

    def stage_b(p):
        b, h = pairs[p]
        r0, c0, l2 = h * 128, b * T, p % 2
        for g in range(4):
            gg = (g*512, (g+1)*512)
            for lh, lkey, srcb, skey, dstb, dkey, pk in (
                    (cx.CM, "c:CM", qhat[l2], f"E2:qhat{l2}", qfre, "E2:qfre", 0),
                    (cx.SM, "c:SM", qhat[l2], f"E2:qhat{l2}", qfim, "E2:qfim", 1),
                    (cx.CM, "c:CM", kcs[l2], f"E2:kcs{l2}", cfre, "E2:cfre", 2),
                    (cx.SM, "c:SM", kcs[l2], f"E2:kcs{l2}", cfim, "E2:cfim", 3)):
                P.t(lambda e, lh=lh, srcb=srcb, gg=gg, pk=pk: e.matmul(
                    cx.ps[pk][:, :], lh[:, :], srcb[:, gg[0]:gg[1]],
                    start=True, stop=True),
                    r=(lkey, skey), w=(f"ps{pk}",))
                P.s(lambda e, dstb=dstb, gg=gg, pk=pk: e.copy(
                    dstb[:, gg[0]:gg[1]], cx.ps[pk][:, :]),
                    r=(f"ps{pk}",), w=(dkey,))
        P.v(lambda e: e.tensor_mul(tm1[:, :], qfre[:, :], cfre[:, :]),
            r=("E2:qfre", "E2:cfre"), w=("E2:tm1",))
        P.g(lambda e: e.tensor_mul(tm2[:, :], qfim[:, :], cfim[:, :]),
            r=("E2:qfim", "E2:cfim"), w=("E2:tm2",))
        P.g(lambda e: e.tensor_mul(tm3[:, :], qfre[:, :], cfim[:, :]),
            r=("E2:qfre", "E2:cfim"), w=("E2:tm3",))
        P.v(lambda e: e.tensor_mul(tm4[:, :], qfim[:, :], cfre[:, :]),
            r=("E2:qfim", "E2:cfre"), w=("E2:tm4",))
        P.v(lambda e: e.tensor_sub(qvre[:, :], tm1[:, :], tm2[:, :]),
            r=("E2:tm1", "E2:tm2"), w=("E2:qvre",))
        P.g(lambda e: e.tensor_add(qvim[:, :], tm3[:, :], tm4[:, :]),
            r=("E2:tm3", "E2:tm4"), w=("E2:qvim",))
        for g in range(4):
            gg = (g*512, (g+1)*512)
            pk = g % 2
            P.t(lambda e, gg=gg, pk=pk: e.matmul(
                cx.ps[pk][:, :], cx.CM2[:, :], qvre[:, gg[0]:gg[1]],
                start=True, stop=False),
                r=("c:CM2", "E2:qvre"), w=(f"ps{pk}",))
            P.t(lambda e, gg=gg, pk=pk: e.matmul(
                cx.ps[pk][:, :], cx.SM2[:, :], qvim[:, gg[0]:gg[1]],
                start=False, stop=True),
                r=("c:SM2", "E2:qvim"), w=(f"ps{pk}",))
            P.s(lambda e, gg=gg, pk=pk: e.copy(
                vhb[:, gg[0]:gg[1]], cx.ps[pk][:, :]),
                r=(f"ps{pk}",), w=("E2:vhb",))
        P.dma("sync", lambda e, r0=r0, c0=c0: e.dma_start(
            io["vh_s"][r0:r0+128, c0:c0+T], vhb[:, :]),
            r=("E2:vhb",), w=())

    emit_loads(0)
    stage_a(0)
    emit_loads(1)
    for p in range(len(pairs)):
        if p + 2 < len(pairs):
            emit_loads(p + 2)
        if p + 1 < len(pairs):
            stage_a(p + 1)
        stage_b(p)


def emit_phaseF(P, cx, stack):
    nc, io = cx.nc, cx.io
    sb = lambda n, s, d: stack.enter_context(nc.sbuf_tensor(n, s, d))
    ysl = sb("F_ysl", [128, NKI * 512], BF16)
    f1stack = ExitStack()
    sb1 = lambda n, s, d: f1stack.enter_context(nc.sbuf_tensor(n, s, d))
    wtbuf = [sb1(f"F_wt{i}", [128, NEXP], F32) for i in range(4)]
    cell = [sb1(f"F_cell{i}", [128, 4], F32) for i in range(4)]
    ebuf = [sb1(f"F_eb{i}", [128, NEXP], F32) for i in range(4)]
    probs = [sb1(f"F_probs{i}", [128, NEXP], F32) for i in range(4)]
    probs15 = [sb1(f"F_p15{i}", [128, NEXP], F32) for i in range(4)]
    biased = [sb1(f"F_biased{i}", [128, NEXP], F32) for i in range(4)]
    mask1 = [sb1(f"F_msk1{i}", [128, NEXP], F32) for i in range(4)]
    mask2 = ebuf
    wtsT = sb1("F_wtsT", [12, NT], F32)
    y_r = io["y3_s"].rearrange("(k p) t -> p k t", p=128)
    out_r = io["out"].rearrange("(k p) t -> p k t", p=128)

    I4 = range(4)
    for tg in range(8):
        c0 = tg * 512
        for q in range(4):
            P.dma("sync", lambda e, q=q, c0=c0: e.dma_start(
                ysl[:, :].rearrange("p (k t) -> p k t", t=512)[:, 4*q:4*q+4, :],
                y_r[:, 4*q:4*q+4, c0:c0+512]),
                w=(f"F:ysl{q}",))
        ykeys = tuple(f"F:ysl{q}" for q in range(4))
        for tc in I4:
            for ki in range(16):
                P.t(lambda e, tc=tc, ki=ki: e.matmul(
                    cx.ps[tc][:, 0:NEXP],
                    ysl[:, ki*512+tc*128:ki*512+(tc+1)*128],
                    cx.cls[:, ki*NEXP:(ki+1)*NEXP],
                    start=(ki == 0), stop=(ki == 15)),
                    r=ykeys + ("c:cls",), w=(f"ps{tc}",))
        for tc in I4:
            P.v(lambda e, tc=tc: e.reduce_max(cell[tc][:, 0:1],
                                              cx.ps[tc][:, 0:NEXP], AX.X),
                r=(f"ps{tc}",), w=(f"F:cell{tc}",))
        for tc in I4:
            P.v(lambda e, tc=tc: e.tensor_scalar_mul(cell[tc][:, 1:2],
                                                     cell[tc][:, 0:1], -1.0),
                r=(f"F:cell{tc}",), w=(f"F:cell{tc}",))
        for tc in I4:
            P.s(lambda e, tc=tc: e.activation(ebuf[tc][:, :],
                                              cx.ps[tc][:, 0:NEXP], AF.Exp,
                                              bias=cell[tc][:, 1:2]),
                r=(f"ps{tc}", f"F:cell{tc}"), w=(f"F:eb{tc}",))
        for tc in I4:
            P.v(lambda e, tc=tc: e.reduce_sum(cell[tc][:, 2:3],
                                              ebuf[tc][:, :], AX.X),
                r=(f"F:eb{tc}",), w=(f"F:cell{tc}",))
        for tc in I4:
            P.v(lambda e, tc=tc: e.reciprocal(cell[tc][:, 3:4],
                                              cell[tc][:, 2:3]),
                r=(f"F:cell{tc}",), w=(f"F:cell{tc}",))
        for tc in I4:
            P.v(lambda e, tc=tc: e.tensor_scalar_mul(probs[tc][:, :],
                                                     ebuf[tc][:, :],
                                                     cell[tc][:, 3:4]),
                r=(f"F:eb{tc}", f"F:cell{tc}"), w=(f"F:probs{tc}",))
        for tc in I4:
            P.v(lambda e, tc=tc: e.tensor_scalar(probs15[tc][:, :],
                                                 ebuf[tc][:, :],
                                                 cell[tc][:, 3:4], 1.5,
                                                 ALU.mult, ALU.mult),
                r=(f"F:eb{tc}", f"F:cell{tc}"), w=(f"F:p15{tc}",))
        for tc in I4:
            P.v(lambda e, tc=tc: e.tensor_add(biased[tc][:, :],
                                              probs[tc][:, :], cx.ebias[:, :]),
                r=(f"F:probs{tc}", "c:ebias"), w=(f"F:biased{tc}",))
        for tc in I4:
            P.v(lambda e, tc=tc: e.reduce_max(cell[tc][:, 0:1],
                                              biased[tc][:, :], AX.X),
                r=(f"F:biased{tc}",), w=(f"F:cell{tc}",))
        for tc in I4:
            P.v(lambda e, tc=tc: e.tensor_scalar_sub(mask1[tc][:, :],
                                                     biased[tc][:, :],
                                                     cell[tc][:, 0:1]),
                r=(f"F:biased{tc}", f"F:cell{tc}"), w=(f"F:m1{tc}",))
        for tc in I4:
            P.v(lambda e, tc=tc: e.tensor_single_scalar(mask1[tc][:, :],
                                                        mask1[tc][:, :],
                                                        0.0, ALU.is_ge),
                r=(f"F:m1{tc}",), w=(f"F:m1{tc}",))
        for tc in I4:
            P.v(lambda e, tc=tc: e.scalar_tensor_tensor(
                biased[tc][:, :], mask1[tc][:, :], -1e30, biased[tc][:, :],
                ALU.mult, ALU.add),
                r=(f"F:m1{tc}", f"F:biased{tc}"), w=(f"F:biased{tc}",))
        for tc in I4:
            P.v(lambda e, tc=tc: e.reduce_max(cell[tc][:, 0:1],
                                              biased[tc][:, :], AX.X),
                r=(f"F:biased{tc}",), w=(f"F:cell{tc}",))
        for tc in I4:
            P.v(lambda e, tc=tc: e.tensor_scalar_sub(mask2[tc][:, :],
                                                     biased[tc][:, :],
                                                     cell[tc][:, 0:1]),
                r=(f"F:biased{tc}", f"F:cell{tc}"),
                w=(f"F:m2{tc}", f"F:eb{tc}"))
        for tc in I4:
            P.v(lambda e, tc=tc: e.tensor_single_scalar(mask2[tc][:, :],
                                                        mask2[tc][:, :],
                                                        0.0, ALU.is_ge),
                r=(f"F:m2{tc}",), w=(f"F:m2{tc}",))
        for tc in I4:
            P.v(lambda e, tc=tc: e.tensor_add(mask1[tc][:, :], mask1[tc][:, :],
                                              mask2[tc][:, :]),
                r=(f"F:m1{tc}", f"F:m2{tc}"), w=(f"F:m1{tc}",))
        for tc in I4:
            P.v(lambda e, tc=tc: e.tensor_mul(wtbuf[tc][:, 0:NEXP],
                                              mask1[tc][:, :],
                                              probs15[tc][:, :]),
                r=(f"F:m1{tc}", f"F:p15{tc}"), w=(f"F:wt{tc}",))
        for tc in I4:
            pk = 4 + tc % 2
            P.t(lambda e, tc=tc, pk=pk: e.transpose(
                cx.ps[pk][0:NEXP, 0:128], wtbuf[tc][:, 0:NEXP],
                cx.ident[:, :]),
                r=(f"F:wt{tc}", "c:ident"), w=(f"ps{pk}",))
            P.v(lambda e, c0=c0, tc=tc, pk=pk: e.tensor_copy(
                wtsT[0:NEXP, c0+tc*128:c0+(tc+1)*128],
                cx.ps[pk][0:NEXP, 0:128]),
                r=(f"ps{pk}",), w=("F:wtsT",))
    P.dma("sync", lambda e: e.dma_start(io["wts_s"][:, :], wtsT[0:NEXP, :]),
          r=("F:wtsT",), w=("F:wts_s",))
    f1stack.close()
    P.barrier()
    webuf = sb("F_webuf", [1, NEXP * 512], BF16)
    z1b = sb("F_z1b", [1, 512], BF16)
    gwb = [sb(f"F_gw{i}", [128, NKI * 128], BF16) for i in range(3)]
    uwb = [sb(f"F_uw{i}", [128, NKI * 128], BF16) for i in range(3)]
    dwb = [sb(f"F_dw{i}", [128, 8 * 128], BF16) for i in range(3)]
    sigb = sb("F_sig", [128, 512], BF16)
    gsb = sb("F_gs", [128, 512], BF16)
    uw2 = sb("F_uw2b", [128, 512], BF16)
    sbwe = [sb(f"F_sbwe{i}", [128, 512], F32) for i in range(2)]
    actb = [sb(f"F_act{i}", [128, 8 * 512], BF16) for i in range(2)]
    ln = LNKit(P, cx, stack, "F", out_f32=True)
    xr = ln.xr

    for tg in range(8):
        c0 = tg * 512
        for q in range(4):
            P.dma("sync", lambda e, q=q, c0=c0: e.dma_start(
                ysl[:, :].rearrange("p (k t) -> p k t", t=512)[:, 4*q:4*q+4, :],
                y_r[:, 4*q:4*q+4, c0:c0+512]),
                w=(f"F:ysl{q}",))
        ykeys = tuple(f"F:ysl{q}" for q in range(4))
        P.dma("gpsimd", lambda e, c0=c0: e.dma_start(
            webuf[0:1, :].rearrange("p (ee t) -> p ee t", t=512),
            io["wts_s"][:, c0:c0+512].rearrange("ee t -> () ee t")),
            r=("F:wts_s",), w=("F:webuf",))
        P.v(lambda e: e.tensor_add(z1b[0:1, :], webuf[0:1, 8*512:9*512],
                                   webuf[0:1, 9*512:10*512]),
            r=("F:webuf",), w=("F:z1b",))
        P.v(lambda e: e.tensor_add(z1b[0:1, :], z1b[0:1, :],
                                   webuf[0:1, 10*512:11*512]),
            r=("F:webuf", "F:z1b"), w=("F:z1b",))
        P.v(lambda e: e.tensor_add(z1b[0:1, :], z1b[0:1, :],
                                   webuf[0:1, 11*512:12*512]),
            r=("F:webuf", "F:z1b"), w=("F:z1b",))
        P.v(lambda e: e.tensor_scalar_add(z1b[0:1, :], z1b[0:1, :], 1.0),
            r=("F:z1b",), w=("F:z1b",))
        P.t(lambda e: e.matmul(cx.ps[4][:, :], cx.ones_row[:, :],
                               z1b[0:1, :], start=True, stop=True),
            r=("F:z1b", "c:ones_row"), w=("ps4",))
        for mo in range(16):
            P.v(lambda e, mo=mo: e.tensor_mul(
                xr[:, mo*512:(mo+1)*512], ysl[:, mo*512:(mo+1)*512],
                cx.ps[4][:, :]),
                r=ykeys + ("ps4",), w=(f"F:xr{mo}",))
        def stage_gu(ex):
            al = ex % 2
            P.t(lambda e, ex=ex: e.matmul(
                cx.ps[5][:, :], cx.ones_row[:, :],
                webuf[0:1, ex*512:(ex+1)*512], start=True, stop=True),
                r=("F:webuf", "c:ones_row"), w=("ps5",))
            P.v(lambda e, al=al: e.tensor_copy(sbwe[al][:, :], cx.ps[5][:, :]),
                r=("ps5",), w=(f"F:sbwe{al}",))
            for i_o in range(8):
                sl = (ex * 8 + i_o) % 3
                P.dma("scalar", lambda e, ex=ex, i_o=i_o, sl=sl: e.dma_start(
                    gwb[sl][:, :].rearrange("p (k j) -> p k j", j=128),
                    io["gw_b"][ex, i_o]),
                    r=("wc:gw_b",), w=(f"F:gw{sl}",))
                P.dma("scalar", lambda e, ex=ex, i_o=i_o, sl=sl: e.dma_start(
                    uwb[sl][:, :].rearrange("p (k j) -> p k j", j=128),
                    io["uw_b"][ex, i_o]),
                    r=("wc:uw_b",), w=(f"F:uw{sl}",))
                pg = i_o % 2
                pu = 2 + i_o % 2
                for ki in range(16):
                    P.t(lambda e, ki=ki, sl=sl, pg=pg: e.matmul(
                        cx.ps[pg][:, :], gwb[sl][:, ki*128:(ki+1)*128],
                        ysl[:, ki*512:(ki+1)*512],
                        start=(ki == 0), stop=(ki == 15)),
                        r=(f"F:gw{sl}",) + ykeys, w=(f"ps{pg}",))
                for ki in range(16):
                    P.t(lambda e, ki=ki, sl=sl, pu=pu: e.matmul(
                        cx.ps[pu][:, :], uwb[sl][:, ki*128:(ki+1)*128],
                        ysl[:, ki*512:(ki+1)*512],
                        start=(ki == 0), stop=(ki == 15)),
                        r=(f"F:uw{sl}",) + ykeys, w=(f"ps{pu}",))
                P.s(lambda e, pg=pg: e.activation(
                    sigb[:, :], cx.ps[pg][:, :], AF.Sigmoid),
                    r=(f"ps{pg}",), w=("F:sig",))
                P.v(lambda e, pg=pg: e.tensor_mul(
                    gsb[:, :], cx.ps[pg][:, :], sigb[:, :]),
                    r=(f"ps{pg}", "F:sig"), w=("F:gs",))
                P.v(lambda e, pu=pu, al=al: e.tensor_mul(
                    uw2[:, :], cx.ps[pu][:, :], sbwe[al][:, :]),
                    r=(f"ps{pu}", f"F:sbwe{al}"), w=("F:uw2",))
                P.v(lambda e, i_o=i_o, al=al: e.tensor_mul(
                    actb[al][:, i_o*512:(i_o+1)*512], gsb[:, :], uw2[:, :]),
                    r=("F:gs", "F:uw2"), w=(f"F:act{al}_{i_o}",))

        def stage_down(ex):
            al = ex % 2
            actkeys = tuple(f"F:act{al}_{i}" for i in range(8))
            for mo in range(16):
                sl = (ex * 16 + mo) % 3
                pk = 6 + mo % 2
                P.dma("gpsimd", lambda e, ex=ex, mo=mo, sl=sl: e.dma_start(
                    dwb[sl][:, :].rearrange("p (k j) -> p k j", j=128),
                    io["dw_b"][ex, mo]),
                    r=("wc:dw_b",), w=(f"F:dw{sl}",))
                for ki in range(8):
                    P.t(lambda e, ki=ki, sl=sl, pk=pk, al=al: e.matmul(
                        cx.ps[pk][:, :], dwb[sl][:, ki*128:(ki+1)*128],
                        actb[al][:, ki*512:(ki+1)*512],
                        start=(ki == 0), stop=(ki == 7)),
                        r=(f"F:dw{sl}",) + actkeys, w=(f"ps{pk}",))
                P.v(lambda e, mo=mo, pk=pk: e.tensor_add(
                    xr[:, mo*512:(mo+1)*512], xr[:, mo*512:(mo+1)*512],
                    cx.ps[pk][:, :]),
                    r=(f"ps{pk}", f"F:xr{mo}"), w=(f"F:xr{mo}",))

        stage_gu(0)
        for ex in range(NE):
            if ex + 1 < NE:
                stage_gu(ex + 1)
            stage_down(ex)
        ln.emit(c0, out_r)



PHASE_IO = {
    "A": (("xT",), ("q_s", "k_s", "v_s")),
    "B": (("q_s", "k_s", "v_s"), ("attn_s",)),
    "C": (("attn_s", "xT"), ("y1_s",)),
    "D1": (("y1_s",), ("act_s",)),
    "D3": (("act_s", "y1_s"), ("y2_s",)),
    "E1": (("y2_s",), ("q2_s", "kv2_s")),
    "E2": (("q2_s", "kv2_s"), ("vh_s",)),
    "E3": (("vh_s", "y2_s"), ("y3_s",)),
    "F": (("y3_s",), ("wts_s", "out")),
}

PHASE_EMIT = {}


def register_phases():
    PHASE_EMIT["A"] = emit_phaseA
    PHASE_EMIT["B"] = emit_phaseB
    PHASE_EMIT["C"] = lambda P, cx, st: emit_out_proj_ln(
        P, cx, st, w_t="wo_t", src="attn_s", resid="xT", dst="y1_s",
        use_bias=True)
    PHASE_EMIT["E3"] = lambda P, cx, st: emit_out_proj_ln(
        P, cx, st, w_t="l1wo_t", src="vh_s", resid="y2_s", dst="y3_s",
        use_bias=False)
    for nm, ph in (("emit_phaseD1", "D1"), ("emit_phaseD3", "D3"),
                   ("emit_phaseE1", "E1"), ("emit_phaseE2", "E2"),
                   ("emit_phaseF", "F")):
        if nm in globals():
            PHASE_EMIT[ph] = globals()[nm]


def run_phases_launch(phase_names, host_arrays, percore, trace=False):
    stack = ExitStack()
    try:
        nc, cx = setup_program(stack, phase_names, set(SCRATCH_SPECS))
        P = Prog()
        emit_init(P, cx)
        P.barrier()
        for ph in phase_names:
            PHASE_EMIT[ph](P, cx, stack)
            P.barrier()
        P.emit(nc, stack)
    finally:
        pass
    reads, writes = set(), set()
    for ph in phase_names:
        r, w = PHASE_IO[ph]
        reads |= set(r)
        writes |= set(w)
    in_maps = []
    for c in range(8):
        m = {}
        m["xT"] = percore[c]["xT"]
        for nm in SHARED_SPECS:
            m[nm] = host_arrays[nm]
        for nm in SCRATCH_SPECS:
            if nm in reads and nm not in writes:
                m[nm] = percore[c][nm]
        in_maps.append(m)
    res = run_bass_kernel_spmd(nc, in_maps, core_ids=list(range(8)),
                               trace=trace)
    stack.close()
    for c in range(8):
        for nm, arr in res.results[c].items():
            percore[c][nm] = arr
    return res


PHASE_ORDER = ("A", "B", "C", "D1", "D3", "E1", "E2", "E3", "F")
FUSED = True


def run_fused(host_arrays, percore, trace=False):
    stack = ExitStack()
    try:
        nc, cx = setup_program(stack, list(PHASE_ORDER), set())
        P = Prog()
        emit_init(P, cx)
        P.barrier()
        for ph in PHASE_ORDER:
            with ExitStack() as phstack:
                PHASE_EMIT[ph](P, cx, phstack)
            P.barrier()
        P.emit(nc, stack)
        in_maps = []
        for c in range(8):
            m = {"xT": percore[c]["xT"]}
            for nm in SHARED_SPECS:
                m[nm] = host_arrays[nm]
            in_maps.append(m)
        res = run_bass_kernel_spmd(nc, in_maps, core_ids=list(range(8)),
                                   trace=trace)
    finally:
        stack.close()
    for c in range(8):
        for nm, arr in res.results[c].items():
            percore[c][nm] = arr
    return res


def kernel(**inputs):
    register_phases()
    sh, percore = prep_host(inputs)
    if FUSED:
        run_fused(sh, percore)
    else:
        for ph in PHASE_ORDER:
            run_phases_launch([ph], sh, percore)
    out = np.zeros((16, 2048, 2048), np.float32)
    for c in range(8):
        r = percore[c]["out"]
        for bl in range(NB):
            out[2 * c + bl] = r[:, bl * T:(bl + 1) * T].T
    return out
